# version 15
# baseline (speedup 1.0000x reference)
import numpy as np
import concourse.bass as bass
import concourse.mybir as mybir
from concourse.bass_utils import run_bass_kernel_spmd
from contextlib import ExitStack

F32 = mybir.dt.float32
BF16 = mybir.dt.bfloat16
I32 = mybir.dt.int32
A = mybir.AluOpType
ACT = mybir.ActivationFunctionType
AX = mybir.AxisListType

NCORES = 8
T = 2048
D = 2048
NIC = 16
NH = 16
NOC = 48
FF = 8192
NJC = 64
CSC = float(128.0 ** 0.5)
EPS = 1e-5
SHRINKS = [float(s) for s in np.linspace(0.4, 1.0, 16)]
QCLIP = 15.4

FCT = 2048 * 128
PJT = 8192 * 128
SLAB = 8 * FCT + 2 * PJT


def build(upto="p4"):
    nc = bass.Bass("TRN2", target_bir_lowering=False)
    nc.detect_race_conditions = False
    x_dt = nc.dram_tensor("x_dt", [D, T], F32, kind="ExternalInput").ap()
    wqkvt = nc.dram_tensor("wqkvt", [NOC, D, 128], F32, kind="ExternalInput").ap()
    bqkv = nc.dram_tensor("bqkv", [NOC, 128], F32, kind="ExternalInput").ap()
    wprojt = nc.dram_tensor("wprojt", [16, D, 128], F32, kind="ExternalInput").ap()
    bproj = nc.dram_tensor("bproj", [16, 128], F32, kind="ExternalInput").ap()
    wfc_sl = nc.dram_tensor("wfc_sl", [1024, 2048], F32, kind="ExternalInput").ap()
    wpj_sl = nc.dram_tensor("wpj_sl", [256, 8192], F32, kind="ExternalInput").ap()
    out_dt = nc.dram_tensor("out_dt", [D, T], F32, kind="ExternalOutput").ap()

    qkv_scr = nc.dram_tensor("qkv_scr", [NOC, 128, T], F32, kind="Internal").ap()
    h2_scr = nc.dram_tensor("h2_scr", [NJC, 128, T], F32, kind="Internal").ap()
    ao_scr = nc.dram_tensor("ao_scr", [NH, 128, T], F32, kind="Internal").ap()
    ap_scr = nc.dram_tensor("ap_scr", [16, 128, T], F32, kind="Internal").ap()
    deq_nat = nc.dram_tensor("deq_nat", [32, 128, 1024], F32, kind="Internal").ap()
    FCS = 8 * FCT
    PJS = 2 * PJT
    deq_in_fc = nc.dram_tensor("deq_in_fc", [1, FCS], F32, kind="Internal").ap()
    deq_in_pj = nc.dram_tensor("deq_in_pj", [1, PJS], F32, kind="Internal").ap()
    deq_all_fc = nc.dram_tensor("deq_all_fc", [NCORES, FCS], F32, kind="Internal").ap()
    deq_all_pj = nc.dram_tensor("deq_all_pj", [NCORES, PJS], F32, kind="Internal").ap()
    deq_fc_f = deq_all_fc.rearrange("c s -> (c s)")
    deq_pj_f = deq_all_pj.rearrange("c s -> (c s)")

    st = ExitStack()
    with st:
        def sem(name):
            return st.enter_context(nc.semaphore(name))

        ONES = st.enter_context(nc.sbuf_tensor("ONES", [128, 128], F32))
        ONESB = st.enter_context(nc.sbuf_tensor("ONESB", [1, 128], BF16))
        ONESB2 = st.enter_context(nc.sbuf_tensor("ONESB2", [128, 128], BF16))
        IDENT = st.enter_context(nc.sbuf_tensor("IDENT", [128, 128], F32))
        IDENTB = st.enter_context(nc.sbuf_tensor("IDENTB", [128, 128], BF16))
        SPC = st.enter_context(nc.sbuf_tensor("SPC", [128, 128], F32))
        BQ = st.enter_context(nc.sbuf_tensor("BQ", [128, NOC], F32))
        BP = st.enter_context(nc.sbuf_tensor("BP", [128, 16], F32))

        s_ph = sem("s_ph")
        s_la = sem("s_la")
        s_lb = sem("s_lb")
        s_lc = sem("s_lc")
        s_cdma = sem("s_cdma")
        s_gdma = sem("s_gdma")
        s_t = sem("s_t")
        s_t2 = sem("s_t2")
        s_v = sem("s_v")
        s_v2 = sem("s_v2")
        s_s = sem("s_s")
        s_g = sem("s_g")
        s_s2 = sem("s_s2")
        s_cc = sem("s_cc")
        s_gst = [sem("s_gst0"), sem("s_gst1")]
        s_stp = [sem("s_st0"), sem("s_st1")]
        s_lx = [sem("s_lx0"), sem("s_lx1")]
        s_gld = [sem("s_gld0"), sem("s_gld1")]

        n = dict(la=0, lb=0, lc=0, cdma=0, gdma=0, t=0, t2=0, v=0, v2=0,
                 s=0, g=0, ph=0)

        block = st.enter_context(nc.Block())

        def barrier():
            n["ph"] += 5
            tgt = n["ph"]
            def _fs(eng, tgt=tgt):
                eng.sem_inc(s_ph, 1)
                eng.wait_ge(s_ph, tgt)
            def _fd(eng, tgt=tgt):
                eng.drain().then_inc(s_ph, 1)
                eng.wait_ge(s_ph, tgt)
            block.sync(_fs)
            for engname in ("gpsimd", "tensor", "vector", "scalar"):
                getattr(block, engname)(_fd)

        def dump_and_done(chunks):
            @block.scalar
            def _(s):
                for i, srcap in enumerate(chunks[:16]):
                    w = srcap.shape[-1]
                    s.dma_start(out=out_dt[i * 128:(i + 1) * 128, 0:w],
                                in_=srcap).then_inc(s_cdma, 16)
                    n["cdma"] += 16
                s.wait_ge(s_cdma, n["cdma"])

        _sc = nc.named_scope("P0"); _sc.__enter__()
        p02 = ExitStack()
        XN = p02.enter_context(nc.sbuf_tensor("XN", [128, NIC * T], F32))
        p0 = ExitStack()
        XA = [p0.enter_context(nc.sbuf_tensor(f"XA{i}", [128, T], F32)) for i in range(2)]
        XSQ = p0.enter_context(nc.sbuf_tensor("XSQ", [128, T], F32))
        BSTG = p0.enter_context(nc.sbuf_tensor("BSTG", [48, 256], F32))
        EPSC = p0.enter_context(nc.sbuf_tensor("EPSC", [128, 1], F32))
        MBT = p0.enter_context(nc.sbuf_tensor("MBT", [128, T], F32))
        RBT = p0.enter_context(nc.sbuf_tensor("RBT", [128, T], F32))
        PP0 = p0.enter_context(nc.psum_tensor("PP0", [128, T], F32))
        PP1 = p0.enter_context(nc.psum_tensor("PP1", [128, T], F32))
        BTR = PP1[:, 0:64]

        @block.gpsimd
        def _(g):
            g.memset(ONES[:, :], 1.0).then_inc(s_g, 1)
            g.wait_ge(s_g, 1)
            g.affine_select(out=IDENT[:, :], in_=ONES[:, :], pattern=[[1, 128]],
                            compare_op=A.is_equal, fill=0.0,
                            base=0, channel_multiplier=-1).then_inc(s_g, 1)
            g.wait_ge(s_g, 2)
            g.tensor_copy(IDENTB[:, :], IDENT[:, :]).then_inc(s_g, 1)
            g.tensor_copy(ONESB[:, :], ONES[0:1, :]).then_inc(s_g, 1)
            g.tensor_copy(ONESB2[:, :], ONES[:, :]).then_inc(s_g, 1)
            g.memset(EPSC[:, :], EPS).then_inc(s_g, 1)
            g.wait_ge(s_g, 6)
            g.sem_inc(s_g, 1)
        n["g"] += 7

        if upto == "p0a":
            dump_and_done([IDENT[:, :], ONES[:, :]])
            p0.close()
            p02.close()
            return nc

        lc0 = n["lc"]

        @block.sync
        def _(sy):
            sy.dma_start(out=BSTG[0:48, 0:128], in_=bqkv[:, :]).then_inc(s_lc, 16)
            sy.dma_start(out=BSTG[0:16, 128:256], in_=bproj[:, :]).then_inc(s_lc, 16)
        n["lc"] += 32

        t2_0 = n["t2"]
        s_0 = n["s"]

        @block.sync
        def _(sy):
            for ic in range(NIC):
                sl = ic % 2
                ssem = s_la if sl == 0 else s_lb
                if ic >= 2:
                    sy.wait_ge(s_t2, t2_0 + ic - 1)
                sy.dma_start(out=XA[sl][:, :],
                             in_=x_dt[ic * 128:(ic + 1) * 128, :]).then_inc(ssem, 16)
                n["la" if sl == 0 else "lb"] += 16

        @block.scalar
        def _(s):
            for ic in range(NIC):
                sl = ic % 2
                ssem = s_la if sl == 0 else s_lb
                s.wait_ge(ssem, 16 * (ic // 2 + 1))
                if ic >= 1:
                    s.wait_ge(s_t2, t2_0 + ic)
                s.activation(XSQ[:, :], XA[sl][:, :], ACT.Square).then_inc(s_s, 1)
                n["s"] += 1

        @block.tensor
        def _(t):
            t.wait_ge(s_g, 7)
            for ic in range(NIC):
                sl = ic % 2
                ssem = s_la if sl == 0 else s_lb
                t.wait_ge(ssem, 16 * (ic // 2 + 1))
                for tt in range(4):
                    t.matmul(PP0[:, tt * 512:(tt + 1) * 512],
                             lhsT=ONES[:, :],
                             rhs=XA[sl][:, tt * 512:(tt + 1) * 512],
                             start=(ic == 0), stop=(ic == NIC - 1))
                t.wait_ge(s_s, s_0 + ic + 1)
                last = None
                for tt in range(4):
                    last = t.matmul(PP1[:, tt * 512:(tt + 1) * 512],
                                    lhsT=ONES[:, :],
                                    rhs=XSQ[:, tt * 512:(tt + 1) * 512],
                                    start=(ic == 0), stop=(ic == NIC - 1))
                last.then_inc(s_t2, 1)
                n["t2"] += 1

        if upto == "p0c":
            dump_and_done([XSQ[:, :], XA[0][:, :], XA[1][:, :]])
            p0.close()
            p02.close()
            return nc

        v_st = n["v"]

        @block.vector
        def _(v):
            v.wait_ge(s_t2, t2_0 + NIC)
            v.tensor_scalar(MBT[:, :], PP0[:, :], 1.0 / 2048.0, None, A.mult)
            v.tensor_scalar(RBT[:, :], PP1[:, :], 1.0 / 2048.0, None, A.mult)
            v.tensor_tensor(XSQ[:, :], MBT[:, :], MBT[:, :], A.mult)
            v.tensor_tensor(RBT[:, :], RBT[:, :], XSQ[:, :],
                            A.subtract).then_inc(s_v, 1)
            n["v"] += 1

        @block.scalar
        def _(s):
            s.wait_ge(s_v, v_st + 1)
            s.activation(RBT[:, :], RBT[:, :], ACT.Sqrt,
                         bias=EPSC[:, 0:1]).then_inc(s_s, 1)
        n["s"] += 1

        s_st = n["s"]

        @block.vector
        def _(v):
            v.wait_ge(s_s, s_st)
            v.reciprocal(RBT[:, :], RBT[:, :]).then_inc(s_v, 1)
        n["v"] += 1
        v_rdy = n["v"]

        if upto == "p0d":
            barrier()
            dump_and_done([MBT[:, :], RBT[:, :]])
            p0.close()
            p02.close()
            return nc

        t_bc = n["t"]

        @block.tensor
        def _(t):
            t.wait_ge(s_lc, lc0 + 32)
            t.wait_ge(s_v, v_rdy)
            t.transpose(BTR[0:128, 0:48], BSTG[0:48, 0:128], IDENT[0:48, 0:48])
            t.transpose(BTR[0:128, 48:64], BSTG[0:16, 128:256],
                        IDENT[0:16, 0:16]).then_inc(s_t, 1)
            n["t"] += 1

        v2_bc = n["v2"]

        @block.vector
        def _(v):
            v.wait_ge(s_t, t_bc + 1)
            v.tensor_copy(BQ[:, :], BTR[:, 0:48])
            v.tensor_copy(BP[:, :], BTR[:, 48:64]).then_inc(s_v2, 1)
            n["v2"] += 1

        if upto == "p0b":
            dump_and_done([MBT[:, :], RBT[:, :], BQ[:, :], BP[:, :]])
            p0.close()
            p02.close()
            return nc

        la_x2 = n["la"]
        lb_x2 = n["lb"]
        v_x2 = n["v"]

        @block.sync
        def _(sy):
            for ic in range(NIC):
                sl = ic % 2
                ssem = s_la if sl == 0 else s_lb
                if ic < 2:
                    sy.wait_ge(s_v2, v2_bc + 1)
                else:
                    sy.wait_ge(s_v, v_x2 + ic - 1)
                sy.dma_start(out=XA[sl][:, :],
                             in_=x_dt[ic * 128:(ic + 1) * 128, :]).then_inc(ssem, 16)
                n["la" if sl == 0 else "lb"] += 16

        @block.vector
        def _(v):
            for ic in range(NIC):
                sl = ic % 2
                ssem = s_la if sl == 0 else s_lb
                base = la_x2 if sl == 0 else lb_x2
                v.wait_ge(ssem, base + 16 * (ic // 2 + 1))
                v.tensor_tensor(XA[sl][:, :], XA[sl][:, :], MBT[:, :], A.subtract)
                v.tensor_tensor(XN[:, ic * T:(ic + 1) * T], XA[sl][:, :], RBT[:, :],
                                A.mult).then_inc(s_v, 1)
                n["v"] += 1

        barrier()
        p0.close()

        if upto == "p0":
            dump_and_done([XN[:, ic * T:(ic + 1) * T] for ic in range(16)])
            p02.close()
            return nc

        _sc.__exit__(None, None, None)
        _sc = nc.named_scope("P1"); _sc.__enter__()
        p1 = ExitStack()
        WQ = [p1.enter_context(nc.sbuf_tensor(f"WQ{i}", [128, NIC * 128], F32)) for i in range(2)]
        QS = p1.enter_context(nc.sbuf_tensor("QS", [128, T], F32))
        QP = p1.enter_context(nc.psum_tensor("QP", [128, T], F32))
        DW = [p1.enter_context(nc.sbuf_tensor(f"DW{i}", [128, 1024], F32)) for i in range(2)]
        DU = p1.enter_context(nc.sbuf_tensor("DU", [128, 1024], F32))
        DY = [p1.enter_context(nc.sbuf_tensor(f"DY{i}", [128, 1024], F32)) for i in range(3)]
        DQI = [p1.enter_context(nc.sbuf_tensor(f"DQI{i}", [128, 1024], I32)) for i in range(3)]
        DC = p1.enter_context(nc.sbuf_tensor("DC", [128, 1024], F32))
        DE = p1.enter_context(nc.sbuf_tensor("DE", [128, 1024], F32))
        DDQ = [p1.enter_context(nc.sbuf_tensor(f"DDQ{i}", [128, 1024], F32)) for i in range(2)]
        BASE = [p1.enter_context(nc.sbuf_tensor(f"BASE{i}", [128, 16], F32)) for i in range(2)]
        BASEr = p1.enter_context(nc.sbuf_tensor("BASEr", [128, 16], F32))
        RBSP = [p1.enter_context(nc.sbuf_tensor(f"RBSP{i}", [128, 16], F32)) for i in range(2)]
        RBSF = p1.enter_context(nc.sbuf_tensor("RBSF", [128, 16], F32))
        ERR = [p1.enter_context(nc.sbuf_tensor(f"ERR{i}", [128, 16], F32)) for i in range(2)]
        BER = p1.enter_context(nc.sbuf_tensor("BER", [128, 16], F32))
        BSS = p1.enter_context(nc.sbuf_tensor("BSS", [128, 16], F32))
        UPD = p1.enter_context(nc.sbuf_tensor("UPD", [128, 16], F32))
        TD1 = p1.enter_context(nc.sbuf_tensor("TD1", [128, 16], F32))
        TD2 = p1.enter_context(nc.sbuf_tensor("TD2", [128, 16], F32))
        TD3 = p1.enter_context(nc.sbuf_tensor("TD3", [128, 16], F32))
        TD4 = p1.enter_context(nc.sbuf_tensor("TD4", [128, 16], F32))

        la_p1 = n["la"]
        lb_p1 = n["lb"]
        t_p1 = n["t"]
        s_p1 = n["s"]
        cd_p1 = n["cdma"]

        @block.sync
        def _(sy):
            for oc in range(NOC):
                sl = oc % 2
                ssem = s_la if sl == 0 else s_lb
                if oc >= 2:
                    sy.wait_ge(s_t, t_p1 + oc - 1)
                sy.dma_start(
                    out=WQ[sl][:, :].rearrange("p (c m) -> p c m", m=128),
                    in_=wqkvt[oc].rearrange("(c p) m -> p c m", p=128),
                ).then_inc(ssem, 16)
                n["la" if sl == 0 else "lb"] += 16

        @block.tensor
        def _(t):
            for oc in range(NOC):
                sl = oc % 2
                ssem = s_la if sl == 0 else s_lb
                base = la_p1 if sl == 0 else lb_p1
                t.wait_ge(ssem, base + 16 * (oc // 2 + 1))
                if oc >= 1:
                    t.wait_ge(s_s, s_p1 + oc)
                last = None
                for tt in range(4):
                    for ic in range(NIC):
                        last = t.matmul(QP[:, tt * 512:(tt + 1) * 512],
                                 lhsT=WQ[sl][:, ic * 128:(ic + 1) * 128],
                                 rhs=XN[:, ic * T + tt * 512: ic * T + (tt + 1) * 512],
                                 start=(ic == 0), stop=(ic == NIC - 1))
                last.then_inc(s_t, 1)
        n["t"] += NOC


        def deq_src_ap(tile):
            if tile < 16:
                r, sct = tile // 2, tile % 2
                return wfc_sl[r * 128:(r + 1) * 128, sct * 1024:(sct + 1) * 1024]
            tt2 = tile - 16
            r, sct = tt2 // 8, tt2 % 8
            return wpj_sl[r * 128:(r + 1) * 128, sct * 1024:(sct + 1) * 1024]

        g_p1 = n["g"]
        v_p1 = n["v"]
        VPT, GPT, SPT = 51, 17, 32

        def vg(tile, j):
            return v_p1 + VPT * tile + j

        def gg(tile, j):
            return g_p1 + GPT * tile + j

        def sg2(tile, j):
            return SPT * tile + j

        def v_clip(k):
            return k + 2 if k <= 1 else 3 * k - 2

        def v_e(k):
            return 3 * k + 5 if k <= 13 else (46 if k == 14 else 48)

        def v_red(k):
            return 3 * k + 6 if k <= 13 else (47 if k == 14 else 49)

        def z_inc(k):
            return k + 1 if k <= 2 else 2 * k - 1

        def sq_inc(k):
            return 4 + 2 * k if k <= 12 else 17 + k

        @block.gpsimd
        def _(g):
            for tile in range(32):
                sl = tile % 2
                if tile == 0:
                    g.dma_start(out=DW[0][:, :],
                                in_=deq_src_ap(0)).then_inc(s_gld[0], 16)
                if tile + 1 <= 31:
                    if tile >= 1:
                        g.wait_ge(s_v, vg(tile - 1, 1))
                    g.dma_start(out=DW[(tile + 1) % 2][:, :],
                                in_=deq_src_ap(tile + 1)
                                ).then_inc(s_gld[(tile + 1) % 2], 16)
                g.wait_ge(s_gld[sl], 16 * (tile // 2 + 1))
                g.sem_inc(s_g, 1)
                for k in range(16):
                    kr = k % 3
                    g.wait_ge(s_v, vg(tile, v_clip(k)))
                    if k >= 3:
                        g.wait_ge(s_v, vg(tile, v_e(k - 3)))
                    elif tile >= 1:
                        g.wait_ge(s_v, vg(tile - 1, v_e(13 + k)))
                    g.tensor_copy(DQI[kr][:, :], DY[kr][:, :]).then_inc(s_g, 1)
                g.wait_ge(s_v, vg(tile, VPT))
                g.dma_start(out=deq_nat[tile], in_=DDQ[sl][:, :]).then_inc(s_gst[sl], 16)

        @block.vector
        def _(v):
            def sp():
                v.tensor_copy(SPC[:, :], IDENT[:, :])

            def pre_chain(t2, stage):
                s2 = t2 % 2
                if stage == 0:
                    v.wait_ge(s_gld[s2], 16 * (t2 // 2 + 1))
                    v.tensor_reduce(BASEr[:, :].unsqueeze(2),
                                    DW[s2][:, :].rearrange("p (b f) -> p b f", f=64),
                                    AX.X, A.max, apply_absolute_value=True)
                elif stage == 1:
                    v.tensor_scalar(BASE[s2][:, :], BASEr[:, :], 1e-8, 1.0 / 15.0,
                                    A.max, A.mult)
                else:
                    v.reciprocal(RBSP[s2][:, :], BASE[s2][:, :])

            for tile in range(32):
                sl = tile % 2
                if tile == 0:
                    pre_chain(0, 0)
                    sp()
                    pre_chain(0, 1)
                    sp()
                    pre_chain(0, 2)
                    sp()
                else:
                    v.wait_ge(s_s2, sg2(tile - 1, SPT))
                v.tensor_tensor(DU[:, :].rearrange("p (b f) -> p b f", f=64),
                                DW[sl][:, :].rearrange("p (b f) -> p b f", f=64),
                                RBSP[sl][:, :].unsqueeze(2).broadcast_to([128, 16, 64]),
                                A.mult).then_inc(s_v, 1)

                def blend(bk, piece):
                    if bk < 0:
                        return
                    if bk == 0:
                        if piece == 0:
                            v.tensor_copy(BER[:, :], ERR[0][:, :])
                        elif piece == 1:
                            v.tensor_scalar(BSS[:, :], ONES[:, 0:16], SHRINKS[0],
                                            None, A.mult)
                        return
                    eb = ERR[bk % 2]
                    if piece == 0:
                        v.tensor_tensor(UPD[:, :], eb[:, :], BER[:, :], A.is_lt)
                        v.tensor_tensor(TD3[:, :], eb[:, :], BER[:, :], A.subtract)
                    elif piece == 1:
                        v.tensor_tensor(TD3[:, :], UPD[:, :], TD3[:, :], A.mult)
                        v.tensor_tensor(TD1[:, :], UPD[:, :], BSS[:, :], A.mult)
                    elif piece == 2:
                        v.tensor_tensor(BER[:, :], BER[:, :], TD3[:, :], A.add)
                    elif piece == 3:
                        v.scalar_tensor_tensor(TD2[:, :], UPD[:, :], SHRINKS[bk],
                                               TD1[:, :], A.mult, A.subtract)
                    else:
                        v.tensor_tensor(BSS[:, :], BSS[:, :], TD2[:, :], A.add)

                for k in range(18):
                    bk = k - 3
                    if k <= 15:
                        v.wait_ge(s_s2, sg2(tile, z_inc(k)))
                        v.tensor_scalar(DY[k % 3][:, :], DY[k % 3][:, :],
                                        QCLIP + 0.5, -QCLIP + 0.5,
                                        A.min, A.max).then_inc(s_v, 1)
                    if k < 2:
                        continue
                    kc_ = k - 2
                    kr = kc_ % 3
                    blend(bk, 0)
                    v.wait_ge(s_g, gg(tile, 2 + kc_))
                    v.tensor_copy(DC[:, :], DQI[kr][:, :])
                    blend(bk, 1)
                    if kc_ >= 1:
                        v.wait_ge(s_s2, sg2(tile, sq_inc(kc_ - 1)))
                    v.tensor_tensor(DE[:, :], DC[:, :], DY[kr][:, :], A.is_gt)
                    blend(bk, 2)
                    v.tensor_tensor(DC[:, :], DC[:, :], DE[:, :], A.subtract)
                    v.scalar_tensor_tensor(DE[:, :], DC[:, :], SHRINKS[kc_],
                                           DU[:, :], A.mult,
                                           A.subtract).then_inc(s_v, 1)
                    blend(bk, 3)
                    if k == 9 and tile < 31:
                        pre_chain(tile + 1, 0)
                    if k == 11 and tile < 31:
                        pre_chain(tile + 1, 1)
                    if k == 13 and tile < 31:
                        pre_chain(tile + 1, 2)
                    v.wait_ge(s_s2, sg2(tile, sq_inc(kc_)))
                    v.tensor_reduce(ERR[kc_ % 2][:, :].unsqueeze(2),
                                    DE[:, :].rearrange("p (b f) -> p b f", f=64),
                                    AX.X, A.add).then_inc(s_v, 1)
                    if bk >= 1 and k == 17:
                        sp()
                    blend(bk, 4)

                sp()
                v.tensor_tensor(UPD[:, :], ERR[1][:, :], BER[:, :], A.is_lt)
                sp()
                v.tensor_tensor(TD1[:, :], UPD[:, :], BSS[:, :], A.mult)
                sp()
                v.scalar_tensor_tensor(TD2[:, :], UPD[:, :], SHRINKS[15],
                                       TD1[:, :], A.mult, A.subtract)
                sp()
                v.tensor_tensor(BSS[:, :], BSS[:, :], TD2[:, :], A.add)
                sp()
                v.reciprocal(RBSF[:, :], BSS[:, :])
                sp()
                v.wait_ge(s_g, gg(tile, GPT))
                v.tensor_tensor(DY[0][:, :].rearrange("p (b f) -> p b f", f=64),
                                DU[:, :].rearrange("p (b f) -> p b f", f=64),
                                RBSF[:, :].unsqueeze(2).broadcast_to([128, 16, 64]),
                                A.mult)
                v.tensor_scalar(DY[0][:, :], DY[0][:, :], QCLIP, -QCLIP,
                                A.min, A.max)
                v.tensor_scalar(DY[0][:, :], DY[0][:, :], 0.5, None,
                                A.add).then_inc(s_v, 1)
                v.tensor_copy(DQI[0][:, :], DY[0][:, :])
                v.tensor_copy(DY[1][:, :], DQI[0][:, :])
                v.tensor_tensor(DC[:, :], DY[0][:, :], DY[1][:, :], A.subtract)
                v.tensor_scalar(DE[:, :], DC[:, :], 1.0, None, A.is_ge)
                v.tensor_tensor(DY[1][:, :], DY[1][:, :], DE[:, :], A.add)
                v.tensor_scalar(DE[:, :], DC[:, :], 2.0, None, A.is_ge)
                v.tensor_tensor(DY[1][:, :], DY[1][:, :], DE[:, :], A.add)
                v.tensor_scalar(DE[:, :], DC[:, :], 0.0, None, A.is_lt)
                v.tensor_tensor(DY[1][:, :], DY[1][:, :], DE[:, :], A.subtract)
                v.tensor_scalar(DE[:, :], DC[:, :], -1.0, None, A.is_lt)
                v.tensor_tensor(DY[1][:, :], DY[1][:, :], DE[:, :], A.subtract)
                v.tensor_tensor(TD1[:, :], BASE[sl][:, :], BSS[:, :], A.mult)
                sp()
                if tile >= 2:
                    v.wait_ge(s_gst[sl], 16 * (tile // 2))
                v.tensor_tensor(DDQ[sl][:, :].rearrange("p (b f) -> p b f", f=64),
                                DY[1][:, :].rearrange("p (b f) -> p b f", f=64),
                                TD1[:, :].unsqueeze(2).broadcast_to([128, 16, 64]),
                                A.mult).then_inc(s_v, 1)
                n["v"] += VPT
        n["g"] += 32 * GPT

        @block.scalar
        def _(s):
            next_oc = [0]

            def emit_evacs(upto):
                while next_oc[0] < min(upto, NOC):
                    oc = next_oc[0]
                    s.wait_ge(s_t, t_p1 + oc + 1)
                    if oc >= 1:
                        s.wait_ge(s_cdma, cd_p1 + 16 * oc)
                    s.activation(QS[:, :], QP[:, :], ACT.Identity,
                                 bias=BQ[:, oc:oc + 1]).then_inc(s_s, 1)
                    n["s"] += 1
                    s.wait_ge(s_s, n["s"])
                    s.dma_start(out=qkv_scr[oc], in_=QS[:, :]).then_inc(s_cdma, 16)
                    n["cdma"] += 16
                    next_oc[0] += 1

            def zk(tile, k):
                s.activation(DY[k % 3][:, :], DU[:, :], ACT.Copy,
                             bias=0.5, scale=1.0 / SHRINKS[k]).then_inc(s_s2, 1)

            for tile in range(32):
                s.wait_ge(s_v, vg(tile, 1))
                for k3 in range(3):
                    if tile >= 1:
                        s.wait_ge(s_g, gg(tile - 1, 15 + k3))
                    zk(tile, k3)
                for k in range(13):
                    emit_evacs(((tile * 16 + k) * 48) // 512)
                    s.wait_ge(s_v, vg(tile, v_e(k)))
                    if k >= 1:
                        s.wait_ge(s_v, vg(tile, v_red(k - 1)))
                    s.activation(DE[:, :], DE[:, :], ACT.Square).then_inc(s_s2, 1)
                    s.wait_ge(s_g, gg(tile, 2 + k))
                    zk(tile, k + 3)
                for k in (13, 14, 15):
                    emit_evacs(((tile * 16 + k) * 48) // 512)
                    s.wait_ge(s_v, vg(tile, v_e(k)))
                    s.wait_ge(s_v, vg(tile, v_red(k - 1)))
                    s.activation(DE[:, :], DE[:, :], ACT.Square).then_inc(s_s2, 1)
            emit_evacs(NOC)

        barrier()
        p1.close()

        if upto == "p1d":
            dump_and_done([deq_nat[t2].rearrange("p f -> p f") for t2 in range(16)])
            p02.close()
            return nc

        if upto == "p1":
            sel = [0, 1, 2, 3, 16, 17, 18, 19, 32, 33, 34, 35, 12, 28, 44, 47]
            dump_and_done([qkv_scr[c] for c in sel])
            p02.close()
            return nc

        _sc.__exit__(None, None, None)
        _sc = nc.named_scope("P15"); _sc.__enter__()
        p15 = ExitStack()
        DNT = [p15.enter_context(nc.sbuf_tensor(f"DNT{i}", [128, 1024], F32)) for i in range(2)]
        DST = [p15.enter_context(nc.sbuf_tensor(f"DST{i}", [128, 1024], F32)) for i in range(2)]
        DTP = p15.enter_context(nc.psum_tensor("DTP", [128, 1024], F32))

        def slab_off(tile):
            if tile < 16:
                r, sct = tile // 2, tile % 2
                return r * FCT + sct * (128 * 1024)
            tt2 = tile - 16
            r, sct = tt2 // 8, tt2 % 8
            return 8 * FCT + r * PJT + sct * (128 * 1024)

        la_p15 = n["la"]
        lb_p15 = n["lb"]
        t_p15 = n["t"]
        v_p15 = n["v"]
        gd_p15 = n["gdma"]

        @block.sync
        def _(sy):
            for tile in range(32):
                sl = tile % 2
                ssem = s_la if sl == 0 else s_lb
                if tile >= 2:
                    sy.wait_ge(s_t, t_p15 + tile - 1)
                sy.wait_ge(s_gst[sl], 16 * (tile // 2 + 1))
                sy.dma_start(out=DNT[sl][:, :], in_=deq_nat[tile]).then_inc(ssem, 16)
                n["la" if sl == 0 else "lb"] += 16

        @block.tensor
        def _(t):
            for tile in range(32):
                sl = tile % 2
                ssem = s_la if sl == 0 else s_lb
                base = la_p15 if sl == 0 else lb_p15
                t.wait_ge(ssem, base + 16 * (tile // 2 + 1))
                if tile >= 1:
                    t.wait_ge(s_v, v_p15 + tile)
                last = None
                for b in range(8):
                    last = t.transpose(DTP[:, b * 128:(b + 1) * 128],
                                       DNT[sl][:, b * 128:(b + 1) * 128], IDENT[:, :])
                last.then_inc(s_t, 1)
        n["t"] += 32

        @block.vector
        def _(v):
            for tile in range(32):
                sl = tile % 2
                v.wait_ge(s_t, t_p15 + tile + 1)
                if tile >= 2:
                    v.wait_ge(s_gst[sl], 256 + 16 * (tile // 2))
                v.tensor_copy(DST[sl][:, :], DTP[:, :]).then_inc(s_v, 1)
                n["v"] += 1

        @block.gpsimd
        def _(g):
            for tile in range(32):
                sl = tile % 2
                g.wait_ge(s_v, v_p15 + tile + 1)
                off = slab_off(tile)
                dst_t = deq_in_fc if tile < 16 else deq_in_pj
                if tile >= 16:
                    off -= 8 * FCT
                g.dma_start(
                    out=dst_t[0, off:off + 128 * 1024].rearrange("(p f) -> p f", p=128),
                    in_=DST[sl][:, :],
                ).then_inc(s_gst[sl], 16)
                if tile == 15:
                    g.wait_ge(s_gst[0], 256 + 16 * 8)
                    g.wait_ge(s_gst[1], 256 + 16 * 8)
                    g.collective_compute(
                        "AllGather", A.bypass,
                        replica_groups=[list(range(NCORES))],
                        ins=[deq_in_fc[:, :]],
                        outs=[deq_all_fc[:, :]],
                    ).then_inc(s_cc, 1)
            g.wait_ge(s_gst[0], 256 + 16 * 16)
            g.wait_ge(s_gst[1], 256 + 16 * 16)
            g.collective_compute(
                "AllGather", A.bypass,
                replica_groups=[list(range(NCORES))],
                ins=[deq_in_pj[:, :]],
                outs=[deq_all_pj[:, :]],
            ).then_inc(s_cc, 1)
            if upto == "p15":
                g.wait_ge(s_cc, 2)

        barrier()
        p15.close()

        if upto == "p15":
            dump_and_done([deq_fc_f[j * 262144:(j + 1) * 262144]
                           .rearrange("(p f) -> p f", p=128) for j in range(16)])
            p02.close()
            return nc

        _sc.__exit__(None, None, None)
        _sc = nc.named_scope("P2"); _sc.__enter__()
        p2 = ExitStack()
        FCW = [p2.enter_context(nc.sbuf_tensor(f"FCW{i}", [128, NIC * 128], F32)) for i in range(2)]
        FR = [p2.enter_context(nc.sbuf_tensor(f"FR{i}", [128, T], F32)) for i in range(2)]
        FH = [p2.enter_context(nc.sbuf_tensor(f"FH{i}", [128, T], F32)) for i in range(2)]
        FCP = [p2.enter_context(nc.psum_tensor(f"FCP{i}", [128, T], F32)) for i in range(2)]

        la_p2 = n["la"]
        lb_p2 = n["lb"]
        t_p2 = n["t"]
        s_p2 = n["s"]
        v_p2 = n["v"]
        cd_p2 = n["cdma"]

        def fc_tile_ap(jc):
            c, r = jc // 8, jc % 8
            base = c * FCS + r * FCT
            return deq_fc_f[base:base + FCT].rearrange("(s p f) -> p s f",
                                                       s=2, p=128)

        @block.sync
        def _(sy):
            sy.wait_ge(s_cc, 1)
            for jc in range(NJC):
                sl = jc % 2
                ssem = s_la if sl == 0 else s_lb
                if jc >= 2:
                    sy.wait_ge(s_t, t_p2 + jc - 1)
                sy.dma_start(
                    out=FCW[sl][:, :].rearrange("p (s f) -> p s f", s=2),
                    in_=fc_tile_ap(jc),
                ).then_inc(ssem, 16)
                n["la" if sl == 0 else "lb"] += 16

        @block.tensor
        def _(t):
            for jc in range(NJC):
                sl = jc % 2
                ssem = s_la if sl == 0 else s_lb
                base = la_p2 if sl == 0 else lb_p2
                t.wait_ge(ssem, base + 16 * (jc // 2 + 1))
                if jc >= 2:
                    t.wait_ge(s_s, s_p2 + jc - 1)
                last = None
                for tt in range(4):
                    for ic in range(NIC):
                        last = t.matmul(FCP[sl][:, tt * 512:(tt + 1) * 512],
                                 lhsT=FCW[sl][:, ic * 128:(ic + 1) * 128],
                                 rhs=XN[:, ic * T + tt * 512: ic * T + (tt + 1) * 512],
                                 start=(ic == 0), stop=(ic == NIC - 1))
                last.then_inc(s_t, 1)
        n["t"] += NJC

        @block.scalar
        def _(s):
            for jc in range(NJC):
                sl = jc % 2
                s.wait_ge(s_t, t_p2 + jc + 1)
                if jc >= 2:
                    s.wait_ge(s_v, v_p2 + jc - 1)
                s.activation(FR[sl][:, :], FCP[sl][:, :], ACT.Relu).then_inc(s_s, 1)
                n["s"] += 1
                s.wait_ge(s_v, v_p2 + jc + 1)
                s.dma_start(out=h2_scr[jc], in_=FH[sl][:, :]).then_inc(s_stp[sl], 16)

        @block.vector
        def _(v):
            for jc in range(NJC):
                sl = jc % 2
                v.wait_ge(s_s, s_p2 + jc + 1)
                if jc >= 2:
                    v.wait_ge(s_stp[sl], 16 * (jc // 2))
                v.tensor_tensor(FH[sl][:, :], FR[sl][:, :], FR[sl][:, :],
                                A.mult).then_inc(s_v, 1)
                n["v"] += 1

        barrier()
        p2.close()
        p02.close()

        if upto == "p2":
            dump_and_done([h2_scr[j] for j in range(16)])
            return nc

        _sc.__exit__(None, None, None)
        _sc = nc.named_scope("P3"); _sc.__enter__()
        p3 = ExitStack()
        QH = [p3.enter_context(nc.sbuf_tensor(f"QH{i}", [128, T], F32)) for i in range(2)]
        KH = [p3.enter_context(nc.sbuf_tensor(f"KH{i}", [128, T], F32)) for i in range(2)]
        VH = [p3.enter_context(nc.sbuf_tensor(f"VH{i}", [128, T], F32)) for i in range(2)]
        VTM = p3.enter_context(nc.sbuf_tensor("VTM", [128, T], F32))
        PTS = p3.enter_context(nc.sbuf_tensor("PTS", [128, T], F32))
        AOU = p3.enter_context(nc.sbuf_tensor("AOU", [128, T], F32))
        AOH = p3.enter_context(nc.sbuf_tensor("AOH", [128, T], F32))
        DISC = p3.enter_context(nc.sbuf_tensor("DISC", [128, T], F32))
        MCOL = p3.enter_context(nc.sbuf_tensor("MCOL", [128, 1], F32))
        MBF = p3.enter_context(nc.sbuf_tensor("MBF", [128, 1], BF16))
        BIASC = p3.enter_context(nc.sbuf_tensor("BIASC", [128, 1], F32))
        DM = p3.enter_context(nc.sbuf_tensor("DM", [128, 128], BF16))
        DALL = p3.enter_context(nc.sbuf_tensor("DALL", [128, T], F32))
        L16 = p3.enter_context(nc.sbuf_tensor("L16", [128, 16], F32))
        RL16 = p3.enter_context(nc.sbuf_tensor("RL16", [128, 16], F32))
        SP = p3.enter_context(nc.psum_tensor("SP", [128, T], F32))
        STP = p3.enter_context(nc.psum_tensor("STP", [128, T], F32))

        la_p3 = n["la"]
        lb_p3 = n["lb"]
        t_p3 = n["t"]
        t2_p3 = n["t2"]
        v_p3 = n["v"]
        v2_p3 = n["v2"]
        s_p3 = n["s"]
        cd_p3 = n["cdma"]

        def tS(h, qt):
            return t_p3 + 64 * h + (1 if qt == 0 else 4 * qt)

        def tSTa(h, qt):
            return t_p3 + 64 * h + (4 * qt + 2 if qt <= 14 else 62)

        def tST(h, qt):
            return t_p3 + 64 * h + (4 * qt + 3 if qt <= 14 else 63)

        def tPV(h, qt):
            return t_p3 + 64 * h + (4 * qt + 5 if qt <= 14 else 64)

        def v_qt(h, qt, j):
            return v_p3 + (h * 16 + qt) * 2 + j

        def s_qt(h, qt, j):
            return s_p3 + (h * 16 + qt) * 3 + j

        @block.sync
        def _(sy):
            for h in range(NH):
                sl = h % 2
                ssem = s_la if sl == 0 else s_lb
                if h >= 2:
                    sy.wait_ge(s_t, tPV(h - 2, 15))
                sy.dma_start(out=QH[sl][:, :], in_=qkv_scr[h]).then_inc(ssem, 16)
                sy.dma_start(out=KH[sl][:, :], in_=qkv_scr[16 + h]).then_inc(ssem, 16)
                sy.dma_start(out=VH[sl][:, :], in_=qkv_scr[32 + h]).then_inc(ssem, 16)
                n["la" if sl == 0 else "lb"] += 48

        @block.tensor
        def _(t):
            for h in range(NH):
                sl = h % 2
                ssem = s_la if sl == 0 else s_lb
                base = la_p3 if sl == 0 else lb_p3
                t.wait_ge(ssem, base + 48 * (h // 2 + 1))
                if h >= 1:
                    t.wait_ge(s_s, s_qt(h - 1, 15, 3))
                    t.wait_ge(s_v2, v2_p3 + 4 * h)
                last = None
                for b in range(NIC):
                    last = t.transpose(STP[:, b * 128:(b + 1) * 128],
                                       VH[sl][:, b * 128:(b + 1) * 128], IDENT[:, :])
                last.then_inc(s_t2, 1)
                last = None
                for kk in range(4):
                    last = t.matmul(SP[:, kk * 512:(kk + 1) * 512],
                             lhsT=QH[sl][:, 0:128],
                             rhs=KH[sl][:, kk * 512:(kk + 1) * 512],
                             start=True, stop=True)
                last.then_inc(s_t, 1)
                for qt in range(16):
                    t.wait_ge(s_v, v_qt(h, qt, 1))
                    if qt == 0:
                        t.wait_ge(s_v2, v2_p3 + 4 * h + 1)
                    else:
                        t.wait_ge(s_s, s_qt(h, qt - 1, 3))
                        t.wait_ge(s_v, v_qt(h, qt - 1, 2))
                    for half in range(2):
                        k0 = half * 8
                        for kc in range(k0, k0 + 8):
                            t.matmul(STP[:, kc * 128:(kc + 1) * 128],
                                     lhsT=KH[sl][:, kc * 128:(kc + 1) * 128],
                                     rhs=QH[sl][:, qt * 128:(qt + 1) * 128],
                                     start=(kc % 4 == 0), stop=False)
                        last = None
                        for kc in range(k0, k0 + 8):
                            last = t.matmul(STP[:, kc * 128:(kc + 1) * 128],
                                     lhsT=ONESB2[:, :], rhs=DM[:, :],
                                     start=False, stop=(kc % 4 == 3))
                        last.then_inc(s_t, 1)
                    if qt < 15:
                        t.wait_ge(s_s, s_qt(h, qt, 1))
                        last = None
                        for kk in range(4):
                            last = t.matmul(SP[:, kk * 512:(kk + 1) * 512],
                                     lhsT=QH[sl][:, (qt + 1) * 128:(qt + 2) * 128],
                                     rhs=KH[sl][:, kk * 512:(kk + 1) * 512],
                                     start=True, stop=True)
                        last.then_inc(s_t, 1)
                    last = None
                    for half in range(2):
                        k0 = half * 8
                        t.wait_ge(s_s, s_qt(h, qt, 2 + half))
                        for kc in range(k0, k0 + 8):
                            last = t.matmul(STP[:, 0:128],
                                     lhsT=VTM[:, kc * 128:(kc + 1) * 128],
                                     rhs=PTS[:, kc * 128:(kc + 1) * 128],
                                     start=(kc == 0), stop=(kc == NIC - 1))
                    last.then_inc(s_t, 1)
                t.wait_ge(s_v2, v2_p3 + 4 * h + 3)
                t.wait_ge(s_s, s_qt(h, 15, 3))
                last = None
                for tt in range(4):
                    last = t.matmul(SP[:, tt * 512:(tt + 1) * 512],
                             lhsT=ONES[:, :],
                             rhs=DALL[:, tt * 512:(tt + 1) * 512],
                             start=True, stop=True)
                last.then_inc(s_t2, 1)
        n["t"] += NH * 64
        n["t2"] += NH * 2
        assert n["t"] == tPV(NH - 1, 15), (n["t"], tPV(NH - 1, 15))

        @block.vector
        def _(v):
            for h in range(NH):
                v.wait_ge(s_t2, t2_p3 + 2 * h + 1)
                v.tensor_copy(VTM[:, :], STP[:, :]).then_inc(s_v2, 1)
                for qt in range(16):
                    v.wait_ge(s_t, tS(h, qt))
                    if h + qt > 0:
                        pqh, pqt = (h, qt - 1) if qt > 0 else (h - 1, 15)
                        v.wait_ge(s_s, s_qt(pqh, pqt, 1))
                        v.wait_ge(s_t, tST(pqh, pqt))
                    v.tensor_reduce(MCOL[:, :], SP[:, :], AX.X, A.max)
                    v.tensor_copy(SPC[:, :], IDENT[:, :])
                    v.tensor_copy(MBF[:, :], MCOL[:, :])
                    v.tensor_copy(SPC[:, :], IDENT[:, :])
                    v.tensor_scalar(BIASC[:, :], MBF[:, :], -CSC, None, A.mult)
                    v.scalar_tensor_tensor(DM[:, :],
                                           MBF[:, 0:1].broadcast_to([128, 128]),
                                           -1.0, IDENTB[:, :], A.mult,
                                           A.mult).then_inc(s_v, 1)
                    v.wait_ge(s_t, tPV(h, qt))
                    v.tensor_copy(AOU[:, qt * 128:(qt + 1) * 128],
                                  STP[:, 0:128]).then_inc(s_v, 1)
                v.wait_ge(s_s, s_qt(h, 15, 3))
                v.reciprocal(RL16[:, :], L16[:, :]).then_inc(s_v2, 1)
                v.tensor_copy(SPC[:, :], IDENT[:, :])
                if h >= 1:
                    v.wait_ge(s_t2, t2_p3 + 2 * h)
                last = None
                for qt in range(16):
                    last = v.scalar_tensor_tensor(DALL[:, qt * 128:(qt + 1) * 128],
                                           RL16[:, qt:qt + 1].broadcast_to([128, 128]),
                                           0.0, IDENT[:, :], A.bypass, A.mult)
                last.then_inc(s_v2, 1)
                v.wait_ge(s_t2, t2_p3 + 2 * h + 2)
                if h >= 1:
                    v.wait_ge(s_cdma, cd_p3 + 16 * h)
                v.tensor_tensor(AOH[:, :], AOU[:, :], SP[:, :], A.mult).then_inc(s_v2, 1)
        n["v"] += NH * 32
        n["v2"] += NH * 4

        @block.scalar
        def _(s):
            for h in range(NH):
                for qt in range(16):
                    s.wait_ge(s_t, tS(h, qt))
                    s.wait_ge(s_v, v_qt(h, qt, 1))
                    s.activation(DISC[:, :], SP[:, :], ACT.Exp,
                                 bias=BIASC[:, 0:1], scale=CSC,
                                 accum_out=L16[:, qt:qt + 1]).then_inc(s_s, 1)
                    n["s"] += 1
                    s.wait_ge(s_t, tSTa(h, qt))
                    s.activation(PTS[:, 0:1024], STP[:, 0:1024], ACT.Exp,
                                 scale=CSC).then_inc(s_s, 1)
                    n["s"] += 1
                    s.wait_ge(s_t, tST(h, qt))
                    s.activation(PTS[:, 1024:2048], STP[:, 1024:2048], ACT.Exp,
                                 scale=CSC).then_inc(s_s, 1)
                    n["s"] += 1
                s.wait_ge(s_v2, v2_p3 + 4 * (h + 1))
                s.dma_start(out=ao_scr[h], in_=AOH[:, :]).then_inc(s_cdma, 16)
                n["cdma"] += 16

        barrier()
        p3.close()

        if upto == "p3":
            dump_and_done([ao_scr[h] for h in range(16)])
            return nc

        _sc.__exit__(None, None, None)
        _sc = nc.named_scope("P35"); _sc.__enter__()
        p35 = ExitStack()
        AOSB = p35.enter_context(nc.sbuf_tensor("AOSB", [128, 16 * 512], F32))
        WPJ = [p35.enter_context(nc.sbuf_tensor(f"WPJ{i}", [128, NIC * 128], F32)) for i in range(2)]
        PST = [p35.enter_context(nc.sbuf_tensor(f"PST{i}", [128, 512], F32)) for i in range(2)]
        PPP = [p35.enter_context(nc.psum_tensor(f"PPP{i}", [128, 512], F32)) for i in range(2)]

        la_p35 = n["la"]
        lb_p35 = n["lb"]
        lc_p35 = n["lc"]
        t_p35 = n["t"]
        s_p35 = n["s"]
        cd_p35 = n["cdma"]

        @block.sync
        def _(sy):
            for ts in range(4):
                if ts >= 1:
                    sy.wait_ge(s_t, t_p35 + ts * 16)
                for h in range(NH):
                    sy.dma_start(out=AOSB[:, h * 512:(h + 1) * 512],
                                 in_=ao_scr[h][:, ts * 512:(ts + 1) * 512]
                                 ).then_inc(s_lc, 16)
                    n["lc"] += 16
                for oc in range(16):
                    it = ts * 16 + oc
                    sl = it % 2
                    ssem = s_la if sl == 0 else s_lb
                    if it >= 2:
                        sy.wait_ge(s_t, t_p35 + it - 1)
                    sy.dma_start(
                        out=WPJ[sl][:, :].rearrange("p (c m) -> p c m", m=128),
                        in_=wprojt[oc].rearrange("(c p) m -> p c m", p=128),
                    ).then_inc(ssem, 16)
                    n["la" if sl == 0 else "lb"] += 16

        @block.tensor
        def _(t):
            for ts in range(4):
                t.wait_ge(s_lc, lc_p35 + 256 * (ts + 1))
                for oc in range(16):
                    it = ts * 16 + oc
                    sl = it % 2
                    ssem = s_la if sl == 0 else s_lb
                    base = la_p35 if sl == 0 else lb_p35
                    t.wait_ge(ssem, base + 16 * (it // 2 + 1))
                    if it >= 2:
                        t.wait_ge(s_s, s_p35 + it - 1)
                    last = None
                    for ad in range(NIC):
                        last = t.matmul(PPP[sl][:, :],
                                 lhsT=WPJ[sl][:, ad * 128:(ad + 1) * 128],
                                 rhs=AOSB[:, ad * 512:(ad + 1) * 512],
                                 start=(ad == 0), stop=(ad == NIC - 1))
                    last.then_inc(s_t, 1)
        n["t"] += 64

        @block.scalar
        def _(s):
            for it in range(64):
                ts, oc = it // 16, it % 16
                sl = it % 2
                s.wait_ge(s_t, t_p35 + it + 1)
                if it >= 2:
                    s.wait_ge(s_stp[sl], 32 * 16 + 16 * (it // 2))
                s.activation(PST[sl][:, :], PPP[sl][:, :], ACT.Identity,
                             bias=BP[:, oc:oc + 1]).then_inc(s_s, 1)
                n["s"] += 1
                s.wait_ge(s_s, n["s"])
                s.dma_start(out=ap_scr[oc][:, ts * 512:(ts + 1) * 512],
                            in_=PST[sl][:, :]).then_inc(s_stp[sl], 16)

        barrier()
        p35.close()

        if upto == "p35":
            dump_and_done([ap_scr[oc] for oc in range(16)])
            return nc

        _sc.__exit__(None, None, None)
        _sc = nc.named_scope("P4"); _sc.__enter__()
        p4 = ExitStack()
        H2S = p4.enter_context(nc.sbuf_tensor("H2S", [128, NJC * 512], F32))
        WDQ = [p4.enter_context(nc.sbuf_tensor(f"WDQ{i}", [128, 16 * 128], F32)) for i in range(2)]
        XT = [p4.enter_context(nc.sbuf_tensor(f"XT{i}", [128, 512], F32)) for i in range(2)]
        APT = [p4.enter_context(nc.sbuf_tensor(f"APT{i}", [128, 512], F32)) for i in range(2)]
        OST = [p4.enter_context(nc.sbuf_tensor(f"OST{i}", [128, 512], F32)) for i in range(2)]
        P4P = [p4.enter_context(nc.psum_tensor(f"P4P{i}", [128, 512], F32)) for i in range(2)]

        la_p4 = n["la"]
        lb_p4 = n["lb"]
        lc_p4 = n["lc"]
        t_p4 = n["t"]
        t2_p4 = n["t2"]
        v_p4 = n["v"]
        cd_p4 = n["cdma"]

        def pj_quarter_ap(oc, jq):
            c, r = oc // 2, oc % 2
            base = c * PJS + r * PJT + jq * (2 * 128 * 1024)
            return deq_pj_f[base:base + 2 * 128 * 1024].rearrange(
                "(s p f) -> p s f", s=2, p=128)

        def lc_ts(ts):
            return lc_p4 + (ts + 1) * 1024

        @block.sync
        def _(sy):
            sy.wait_ge(s_cc, 2)
            for ts in range(4):
                if ts >= 1:
                    sy.wait_ge(s_t2, t2_p4 + 64 * ts)
                for jc in range(NJC):
                    sy.dma_start(out=H2S[:, jc * 512:(jc + 1) * 512],
                                 in_=h2_scr[jc][:, ts * 512:(ts + 1) * 512]
                                 ).then_inc(s_lc, 16)
                    n["lc"] += 16
                for oc in range(16):
                    it = ts * 16 + oc
                    sl2 = it % 2
                    if it >= 2:
                        sy.wait_ge(s_v, v_p4 + it - 1)
                    sy.dma_start(out=XT[sl2][:, :],
                                 in_=x_dt[oc * 128:(oc + 1) * 128,
                                          ts * 512:(ts + 1) * 512]).then_inc(s_lx[sl2], 16)
                    sy.dma_start(out=APT[sl2][:, :],
                                 in_=ap_scr[oc][:, ts * 512:(ts + 1) * 512]
                                 ).then_inc(s_lx[sl2], 16)
                    for jq in range(4):
                        sq = it * 4 + jq
                        sl = sq % 2
                        ssem = s_la if sl == 0 else s_lb
                        if sq >= 2:
                            sy.wait_ge(s_t2, t2_p4 + sq - 1)
                        sy.dma_start(
                            out=WDQ[sl][:, :].rearrange("p (s f) -> p s f", s=2),
                            in_=pj_quarter_ap(oc, jq),
                        ).then_inc(ssem, 16)
                        n["la" if sl == 0 else "lb"] += 16

        @block.tensor
        def _(t):
            for ts in range(4):
                t.wait_ge(s_lc, lc_ts(ts))
                for oc in range(16):
                    it = ts * 16 + oc
                    sl2 = it % 2
                    if it >= 2:
                        t.wait_ge(s_v, v_p4 + it - 1)
                    for jq in range(4):
                        sq = it * 4 + jq
                        sl = sq % 2
                        ssem = s_la if sl == 0 else s_lb
                        base = la_p4 if sl == 0 else lb_p4
                        t.wait_ge(ssem, base + 16 * (sq // 2 + 1))
                        last = None
                        for jj in range(16):
                            jcg = jq * 16 + jj
                            last = t.matmul(P4P[sl2][:, :],
                                     lhsT=WDQ[sl][:, jj * 128:(jj + 1) * 128],
                                     rhs=H2S[:, jcg * 512:(jcg + 1) * 512],
                                     start=(jq == 0 and jj == 0),
                                     stop=(jq == 3 and jj == 15))
                        last.then_inc(s_t2, 1)
                        n["t2"] += 1

        @block.vector
        def _(v):
            for ts in range(4):
                for oc in range(16):
                    it = ts * 16 + oc
                    sl2 = it % 2
                    v.wait_ge(s_t2, t2_p4 + 4 * (it + 1))
                    v.wait_ge(s_lx[sl2], 32 * (it // 2 + 1))
                    if it >= 2:
                        v.wait_ge(s_stp[sl2], 64 * 16 + 16 * (it // 2))
                    v.tensor_tensor(OST[sl2][:, :], P4P[sl2][:, :], APT[sl2][:, :],
                                    A.add)
                    v.tensor_tensor(OST[sl2][:, :], OST[sl2][:, :], XT[sl2][:, :],
                                    A.add).then_inc(s_v, 1)
                    n["v"] += 1

        @block.scalar
        def _(s):
            for it in range(64):
                ts, oc = it // 16, it % 16
                sl2 = it % 2
                s.wait_ge(s_v, v_p4 + it + 1)
                s.dma_start(out=out_dt[oc * 128:(oc + 1) * 128,
                                       ts * 512:(ts + 1) * 512],
                            in_=OST[sl2][:, :]).then_inc(s_stp[sl2], 16)
            s.wait_ge(s_stp[0], 64 * 16 + 16 * 32)
            s.wait_ge(s_stp[1], 64 * 16 + 16 * 32)

        p4.close()
        _sc.__exit__(None, None, None)
    return nc


_NC_CACHE = None


def kernel(L0_attn_query, L0_attn_key, L0_attn_value, L0_attn_Wqkv, L0_attn_bqkv,
           L0_attn_Wproj, L0_attn_bproj, L0_ffn_w_fc, L0_ffn_w_proj):
    global _NC_CACHE
    x = np.asarray(L0_attn_query, dtype=np.float32)
    Wqkv = np.asarray(L0_attn_Wqkv, dtype=np.float32)
    bq = np.asarray(L0_attn_bqkv, dtype=np.float32)
    Wproj = np.asarray(L0_attn_Wproj, dtype=np.float32)
    bp = np.asarray(L0_attn_bproj, dtype=np.float32)
    w_fc = np.asarray(L0_ffn_w_fc, dtype=np.float32)
    w_proj = np.asarray(L0_ffn_w_proj, dtype=np.float32)

    idx = np.arange(6144).reshape(16, 3, 128)
    perm = np.concatenate([idx[:, 0], idx[:, 1], idx[:, 2]], axis=0).reshape(-1)
    WqkvT_t = np.ascontiguousarray(
        Wqkv[perm].T.reshape(2048, 48, 128).transpose(1, 0, 2))
    bq_t = np.ascontiguousarray(bq[perm].reshape(48, 128))
    WprojT_t = np.ascontiguousarray(
        Wproj.T.reshape(2048, 16, 128).transpose(1, 0, 2))
    bp_t = np.ascontiguousarray(bp.reshape(16, 128))

    if _NC_CACHE is None:
        _NC_CACHE = build()
    nc = _NC_CACHE

    in_maps = []
    for c in range(NCORES):
        in_maps.append({
            "x_dt": np.ascontiguousarray(x[c].T),
            "wqkvt": WqkvT_t,
            "bqkv": bq_t,
            "wprojt": WprojT_t,
            "bproj": bp_t,
            "wfc_sl": np.ascontiguousarray(w_fc[c * 1024:(c + 1) * 1024]),
            "wpj_sl": np.ascontiguousarray(w_proj[c * 256:(c + 1) * 256]),
        })
    res = run_bass_kernel_spmd(nc, in_maps, core_ids=list(range(NCORES)))
    out = np.stack([np.ascontiguousarray(res.results[c]["out_dt"].T)
                    for c in range(NCORES)], axis=0)
    return out


# version 16 (final)
# speedup vs baseline: 1.0279x; 1.0279x over previous
import numpy as np
import concourse.bass as bass
import concourse.mybir as mybir
from concourse.bass_utils import run_bass_kernel_spmd
from contextlib import ExitStack

F32 = mybir.dt.float32
BF16 = mybir.dt.bfloat16
I32 = mybir.dt.int32
A = mybir.AluOpType
ACT = mybir.ActivationFunctionType
AX = mybir.AxisListType

NCORES = 8
T = 2048
D = 2048
NIC = 16
NH = 16
NOC = 48
FF = 8192
NJC = 64
CSC = float(128.0 ** 0.5)
EPS = 1e-5
SHRINKS = [float(s) for s in np.linspace(0.4, 1.0, 16)]
QCLIP = 15.4

FCT = 2048 * 128
PJT = 8192 * 128
SLAB = 8 * FCT + 2 * PJT


def build(upto="p4"):
    nc = bass.Bass("TRN2", target_bir_lowering=False)
    nc.detect_race_conditions = False
    x_dt = nc.dram_tensor("x_dt", [D, T], F32, kind="ExternalInput").ap()
    wqkvt = nc.dram_tensor("wqkvt", [NOC, D, 128], F32, kind="ExternalInput").ap()
    bqkv = nc.dram_tensor("bqkv", [NOC, 128], F32, kind="ExternalInput").ap()
    wprojt = nc.dram_tensor("wprojt", [16, D, 128], F32, kind="ExternalInput").ap()
    bproj = nc.dram_tensor("bproj", [16, 128], F32, kind="ExternalInput").ap()
    wfc_sl = nc.dram_tensor("wfc_sl", [1024, 2048], F32, kind="ExternalInput").ap()
    wpj_sl = nc.dram_tensor("wpj_sl", [256, 8192], F32, kind="ExternalInput").ap()
    out_dt = nc.dram_tensor("out_dt", [D, T], F32, kind="ExternalOutput").ap()

    qkv_scr = nc.dram_tensor("qkv_scr", [NOC, 128, T], F32, kind="Internal").ap()
    h2_scr = nc.dram_tensor("h2_scr", [NJC, 128, T], F32, kind="Internal").ap()
    ao_scr = nc.dram_tensor("ao_scr", [NH, 128, T], F32, kind="Internal").ap()
    ap_scr = nc.dram_tensor("ap_scr", [16, 128, T], F32, kind="Internal").ap()
    deq_nat = nc.dram_tensor("deq_nat", [32, 128, 1024], F32, kind="Internal").ap()
    FCS = 8 * FCT
    PJS = 2 * PJT
    deq_in_fc = nc.dram_tensor("deq_in_fc", [1, FCS], F32, kind="Internal").ap()
    deq_in_pj = nc.dram_tensor("deq_in_pj", [1, PJS], F32, kind="Internal").ap()
    deq_all_fc = nc.dram_tensor("deq_all_fc", [NCORES, FCS], F32, kind="Internal").ap()
    deq_all_pj = nc.dram_tensor("deq_all_pj", [NCORES, PJS], F32, kind="Internal").ap()
    deq_fc_f = deq_all_fc.rearrange("c s -> (c s)")
    deq_pj_f = deq_all_pj.rearrange("c s -> (c s)")

    st = ExitStack()
    with st:
        def sem(name):
            return st.enter_context(nc.semaphore(name))

        ONES = st.enter_context(nc.sbuf_tensor("ONES", [128, 128], F32))
        ONESB = st.enter_context(nc.sbuf_tensor("ONESB", [1, 128], BF16))
        ONESB2 = st.enter_context(nc.sbuf_tensor("ONESB2", [128, 128], BF16))
        IDENT = st.enter_context(nc.sbuf_tensor("IDENT", [128, 128], F32))
        IDENTB = st.enter_context(nc.sbuf_tensor("IDENTB", [128, 128], BF16))
        SPC = st.enter_context(nc.sbuf_tensor("SPC", [128, 128], F32))
        BQ = st.enter_context(nc.sbuf_tensor("BQ", [128, NOC], F32))
        BP = st.enter_context(nc.sbuf_tensor("BP", [128, 16], F32))

        s_ph = sem("s_ph")
        s_la = sem("s_la")
        s_lb = sem("s_lb")
        s_lc = sem("s_lc")
        s_cdma = sem("s_cdma")
        s_gdma = sem("s_gdma")
        s_t = sem("s_t")
        s_t2 = sem("s_t2")
        s_v = sem("s_v")
        s_v2 = sem("s_v2")
        s_s = sem("s_s")
        s_g = sem("s_g")
        s_s2 = sem("s_s2")
        s_cc = sem("s_cc")
        s_gst = [sem("s_gst0"), sem("s_gst1")]
        s_stp = [sem("s_st0"), sem("s_st1")]
        s_lx = [sem("s_lx0"), sem("s_lx1")]
        s_gld = [sem("s_gld0"), sem("s_gld1")]

        n = dict(la=0, lb=0, lc=0, cdma=0, gdma=0, t=0, t2=0, v=0, v2=0,
                 s=0, g=0, ph=0)

        block = st.enter_context(nc.Block())

        def barrier():
            n["ph"] += 5
            tgt = n["ph"]
            def _fs(eng, tgt=tgt):
                eng.sem_inc(s_ph, 1)
                eng.wait_ge(s_ph, tgt)
            def _fd(eng, tgt=tgt):
                eng.drain().then_inc(s_ph, 1)
                eng.wait_ge(s_ph, tgt)
            block.sync(_fs)
            for engname in ("gpsimd", "tensor", "vector", "scalar"):
                getattr(block, engname)(_fd)

        def dump_and_done(chunks):
            @block.scalar
            def _(s):
                for i, srcap in enumerate(chunks[:16]):
                    w = srcap.shape[-1]
                    s.dma_start(out=out_dt[i * 128:(i + 1) * 128, 0:w],
                                in_=srcap).then_inc(s_cdma, 16)
                    n["cdma"] += 16
                s.wait_ge(s_cdma, n["cdma"])

        _sc = nc.named_scope("P0"); _sc.__enter__()
        p02 = ExitStack()
        XN = p02.enter_context(nc.sbuf_tensor("XN", [128, NIC * T], F32))
        p0 = ExitStack()
        XA = [p0.enter_context(nc.sbuf_tensor(f"XA{i}", [128, T], F32)) for i in range(2)]
        XSQ = p0.enter_context(nc.sbuf_tensor("XSQ", [128, T], F32))
        BSTG = p0.enter_context(nc.sbuf_tensor("BSTG", [48, 256], F32))
        EPSC = p0.enter_context(nc.sbuf_tensor("EPSC", [128, 1], F32))
        MBT = p0.enter_context(nc.sbuf_tensor("MBT", [128, T], F32))
        RBT = p0.enter_context(nc.sbuf_tensor("RBT", [128, T], F32))
        PP0 = p0.enter_context(nc.psum_tensor("PP0", [128, T], F32))
        PP1 = p0.enter_context(nc.psum_tensor("PP1", [128, T], F32))
        BTR = PP1[:, 0:64]

        @block.gpsimd
        def _(g):
            g.memset(ONES[:, :], 1.0).then_inc(s_g, 1)
            g.wait_ge(s_g, 1)
            g.affine_select(out=IDENT[:, :], in_=ONES[:, :], pattern=[[1, 128]],
                            compare_op=A.is_equal, fill=0.0,
                            base=0, channel_multiplier=-1).then_inc(s_g, 1)
            g.wait_ge(s_g, 2)
            g.tensor_copy(IDENTB[:, :], IDENT[:, :]).then_inc(s_g, 1)
            g.tensor_copy(ONESB[:, :], ONES[0:1, :]).then_inc(s_g, 1)
            g.tensor_copy(ONESB2[:, :], ONES[:, :]).then_inc(s_g, 1)
            g.memset(EPSC[:, :], EPS).then_inc(s_g, 1)
            g.wait_ge(s_g, 6)
            g.sem_inc(s_g, 1)
        n["g"] += 7

        if upto == "p0a":
            dump_and_done([IDENT[:, :], ONES[:, :]])
            p0.close()
            p02.close()
            return nc

        lc0 = n["lc"]

        @block.sync
        def _(sy):
            sy.dma_start(out=BSTG[0:48, 0:128], in_=bqkv[:, :]).then_inc(s_lc, 16)
            sy.dma_start(out=BSTG[0:16, 128:256], in_=bproj[:, :]).then_inc(s_lc, 16)
        n["lc"] += 32

        t2_0 = n["t2"]
        s_0 = n["s"]

        @block.sync
        def _(sy):
            for ic in range(NIC):
                sl = ic % 2
                ssem = s_la if sl == 0 else s_lb
                if ic >= 2:
                    sy.wait_ge(s_t2, t2_0 + ic - 1)
                sy.dma_start(out=XA[sl][:, :],
                             in_=x_dt[ic * 128:(ic + 1) * 128, :]).then_inc(ssem, 16)
                n["la" if sl == 0 else "lb"] += 16

        @block.scalar
        def _(s):
            for ic in range(NIC):
                sl = ic % 2
                ssem = s_la if sl == 0 else s_lb
                s.wait_ge(ssem, 16 * (ic // 2 + 1))
                if ic >= 1:
                    s.wait_ge(s_t2, t2_0 + ic)
                s.activation(XSQ[:, :], XA[sl][:, :], ACT.Square).then_inc(s_s, 1)
                n["s"] += 1

        @block.tensor
        def _(t):
            t.wait_ge(s_g, 7)
            for ic in range(NIC):
                sl = ic % 2
                ssem = s_la if sl == 0 else s_lb
                t.wait_ge(ssem, 16 * (ic // 2 + 1))
                for tt in range(4):
                    t.matmul(PP0[:, tt * 512:(tt + 1) * 512],
                             lhsT=ONES[:, :],
                             rhs=XA[sl][:, tt * 512:(tt + 1) * 512],
                             start=(ic == 0), stop=(ic == NIC - 1))
                t.wait_ge(s_s, s_0 + ic + 1)
                last = None
                for tt in range(4):
                    last = t.matmul(PP1[:, tt * 512:(tt + 1) * 512],
                                    lhsT=ONES[:, :],
                                    rhs=XSQ[:, tt * 512:(tt + 1) * 512],
                                    start=(ic == 0), stop=(ic == NIC - 1))
                last.then_inc(s_t2, 1)
                n["t2"] += 1

        if upto == "p0c":
            dump_and_done([XSQ[:, :], XA[0][:, :], XA[1][:, :]])
            p0.close()
            p02.close()
            return nc

        v_st = n["v"]

        @block.vector
        def _(v):
            v.wait_ge(s_t2, t2_0 + NIC)
            v.tensor_scalar(MBT[:, :], PP0[:, :], 1.0 / 2048.0, None, A.mult)
            v.tensor_scalar(RBT[:, :], PP1[:, :], 1.0 / 2048.0, None, A.mult)
            v.tensor_tensor(XSQ[:, :], MBT[:, :], MBT[:, :], A.mult)
            v.tensor_tensor(RBT[:, :], RBT[:, :], XSQ[:, :],
                            A.subtract).then_inc(s_v, 1)
            n["v"] += 1

        @block.scalar
        def _(s):
            s.wait_ge(s_v, v_st + 1)
            s.activation(RBT[:, :], RBT[:, :], ACT.Sqrt,
                         bias=EPSC[:, 0:1]).then_inc(s_s, 1)
        n["s"] += 1

        s_st = n["s"]

        @block.vector
        def _(v):
            v.wait_ge(s_s, s_st)
            v.reciprocal(RBT[:, :], RBT[:, :]).then_inc(s_v, 1)
        n["v"] += 1
        v_rdy = n["v"]

        if upto == "p0d":
            barrier()
            dump_and_done([MBT[:, :], RBT[:, :]])
            p0.close()
            p02.close()
            return nc

        t_bc = n["t"]

        @block.tensor
        def _(t):
            t.wait_ge(s_lc, lc0 + 32)
            t.wait_ge(s_v, v_rdy)
            t.transpose(BTR[0:128, 0:48], BSTG[0:48, 0:128], IDENT[0:48, 0:48])
            t.transpose(BTR[0:128, 48:64], BSTG[0:16, 128:256],
                        IDENT[0:16, 0:16]).then_inc(s_t, 1)
            n["t"] += 1

        v2_bc = n["v2"]

        @block.vector
        def _(v):
            v.wait_ge(s_t, t_bc + 1)
            v.tensor_copy(BQ[:, :], BTR[:, 0:48])
            v.tensor_copy(BP[:, :], BTR[:, 48:64]).then_inc(s_v2, 1)
            n["v2"] += 1

        if upto == "p0b":
            dump_and_done([MBT[:, :], RBT[:, :], BQ[:, :], BP[:, :]])
            p0.close()
            p02.close()
            return nc

        la_x2 = n["la"]
        lb_x2 = n["lb"]
        v_x2 = n["v"]

        @block.sync
        def _(sy):
            for ic in range(NIC):
                sl = ic % 2
                ssem = s_la if sl == 0 else s_lb
                if ic < 2:
                    sy.wait_ge(s_v2, v2_bc + 1)
                else:
                    sy.wait_ge(s_v, v_x2 + ic - 1)
                sy.dma_start(out=XA[sl][:, :],
                             in_=x_dt[ic * 128:(ic + 1) * 128, :]).then_inc(ssem, 16)
                n["la" if sl == 0 else "lb"] += 16

        @block.vector
        def _(v):
            for ic in range(NIC):
                sl = ic % 2
                ssem = s_la if sl == 0 else s_lb
                base = la_x2 if sl == 0 else lb_x2
                v.wait_ge(ssem, base + 16 * (ic // 2 + 1))
                v.tensor_tensor(XA[sl][:, :], XA[sl][:, :], MBT[:, :], A.subtract)
                v.tensor_tensor(XN[:, ic * T:(ic + 1) * T], XA[sl][:, :], RBT[:, :],
                                A.mult).then_inc(s_v, 1)
                n["v"] += 1

        barrier()
        p0.close()

        if upto == "p0":
            dump_and_done([XN[:, ic * T:(ic + 1) * T] for ic in range(16)])
            p02.close()
            return nc

        _sc.__exit__(None, None, None)
        _sc = nc.named_scope("P1"); _sc.__enter__()
        p1 = ExitStack()
        WQ = [p1.enter_context(nc.sbuf_tensor(f"WQ{i}", [128, NIC * 128], F32)) for i in range(2)]
        QS = p1.enter_context(nc.sbuf_tensor("QS", [128, T], F32))
        QP = p1.enter_context(nc.psum_tensor("QP", [128, T], F32))
        DW = [p1.enter_context(nc.sbuf_tensor(f"DW{i}", [128, 1024], F32)) for i in range(2)]
        DU = p1.enter_context(nc.sbuf_tensor("DU", [128, 1024], F32))
        DY = [p1.enter_context(nc.sbuf_tensor(f"DY{i}", [128, 1024], F32)) for i in range(3)]
        DQI = [p1.enter_context(nc.sbuf_tensor(f"DQI{i}", [128, 1024], I32)) for i in range(3)]
        DC = p1.enter_context(nc.sbuf_tensor("DC", [128, 1024], F32))
        DE = p1.enter_context(nc.sbuf_tensor("DE", [128, 1024], F32))
        DDQ = [p1.enter_context(nc.sbuf_tensor(f"DDQ{i}", [128, 1024], F32)) for i in range(2)]
        BASE = [p1.enter_context(nc.sbuf_tensor(f"BASE{i}", [128, 16], F32)) for i in range(2)]
        BASEr = p1.enter_context(nc.sbuf_tensor("BASEr", [128, 16], F32))
        RBSP = [p1.enter_context(nc.sbuf_tensor(f"RBSP{i}", [128, 16], F32)) for i in range(2)]
        RBSF = p1.enter_context(nc.sbuf_tensor("RBSF", [128, 16], F32))
        ERR = [p1.enter_context(nc.sbuf_tensor(f"ERR{i}", [128, 16], F32)) for i in range(2)]
        BER = p1.enter_context(nc.sbuf_tensor("BER", [128, 16], F32))
        BSS = p1.enter_context(nc.sbuf_tensor("BSS", [128, 16], F32))
        UPD = p1.enter_context(nc.sbuf_tensor("UPD", [128, 16], F32))
        TD1 = p1.enter_context(nc.sbuf_tensor("TD1", [128, 16], F32))
        TD2 = p1.enter_context(nc.sbuf_tensor("TD2", [128, 16], F32))
        TD3 = p1.enter_context(nc.sbuf_tensor("TD3", [128, 16], F32))
        TD4 = p1.enter_context(nc.sbuf_tensor("TD4", [128, 16], F32))

        la_p1 = n["la"]
        lb_p1 = n["lb"]
        t_p1 = n["t"]
        s_p1 = n["s"]
        cd_p1 = n["cdma"]

        @block.sync
        def _(sy):
            for oc in range(NOC):
                sl = oc % 2
                ssem = s_la if sl == 0 else s_lb
                if oc >= 2:
                    sy.wait_ge(s_t, t_p1 + oc - 1)
                sy.dma_start(
                    out=WQ[sl][:, :].rearrange("p (c m) -> p c m", m=128),
                    in_=wqkvt[oc].rearrange("(c p) m -> p c m", p=128),
                ).then_inc(ssem, 16)
                n["la" if sl == 0 else "lb"] += 16

        @block.tensor
        def _(t):
            for oc in range(NOC):
                sl = oc % 2
                ssem = s_la if sl == 0 else s_lb
                base = la_p1 if sl == 0 else lb_p1
                t.wait_ge(ssem, base + 16 * (oc // 2 + 1))
                if oc >= 1:
                    t.wait_ge(s_s, s_p1 + oc)
                last = None
                for tt in range(4):
                    for ic in range(NIC):
                        last = t.matmul(QP[:, tt * 512:(tt + 1) * 512],
                                 lhsT=WQ[sl][:, ic * 128:(ic + 1) * 128],
                                 rhs=XN[:, ic * T + tt * 512: ic * T + (tt + 1) * 512],
                                 start=(ic == 0), stop=(ic == NIC - 1))
                last.then_inc(s_t, 1)
        n["t"] += NOC


        def deq_src_ap(tile):
            if tile < 16:
                r, sct = tile // 2, tile % 2
                return wfc_sl[r * 128:(r + 1) * 128, sct * 1024:(sct + 1) * 1024]
            tt2 = tile - 16
            r, sct = tt2 // 8, tt2 % 8
            return wpj_sl[r * 128:(r + 1) * 128, sct * 1024:(sct + 1) * 1024]

        g_p1 = n["g"]
        v_p1 = n["v"]
        VPT, GPT, SPT = 51, 17, 32

        def vg(tile, j):
            return v_p1 + VPT * tile + j

        def gg(tile, j):
            return g_p1 + GPT * tile + j

        def sg2(tile, j):
            return SPT * tile + j

        def v_clip(k):
            return k + 2 if k <= 1 else 3 * k - 2

        def v_e(k):
            return 3 * k + 5 if k <= 13 else (46 if k == 14 else 48)

        def v_red(k):
            return 3 * k + 6 if k <= 13 else (47 if k == 14 else 49)

        def z_inc(k):
            return k + 1 if k <= 2 else 2 * k - 1

        def sq_inc(k):
            return 4 + 2 * k if k <= 12 else 17 + k

        @block.gpsimd
        def _(g):
            for tile in range(32):
                sl = tile % 2
                if tile == 0:
                    g.dma_start(out=DW[0][:, :],
                                in_=deq_src_ap(0)).then_inc(s_gld[0], 16)
                if tile + 1 <= 31:
                    if tile >= 1:
                        g.wait_ge(s_v, vg(tile - 1, 1))
                    g.dma_start(out=DW[(tile + 1) % 2][:, :],
                                in_=deq_src_ap(tile + 1)
                                ).then_inc(s_gld[(tile + 1) % 2], 16)
                g.wait_ge(s_gld[sl], 16 * (tile // 2 + 1))
                g.sem_inc(s_g, 1)
                for k in range(16):
                    kr = k % 3
                    g.wait_ge(s_v, vg(tile, v_clip(k)))
                    if k >= 3:
                        g.wait_ge(s_v, vg(tile, v_e(k - 3)))
                    elif tile >= 1:
                        g.wait_ge(s_v, vg(tile - 1, v_e(13 + k)))
                    g.tensor_copy(DQI[kr][:, :], DY[kr][:, :]).then_inc(s_g, 1)
                g.wait_ge(s_v, vg(tile, VPT))
                g.dma_start(out=deq_nat[tile], in_=DDQ[sl][:, :]).then_inc(s_gst[sl], 16)

        @block.vector
        def _(v):
            def sp():
                v.tensor_copy(SPC[:, :], IDENT[:, :])

            def pre_chain(t2, stage):
                s2 = t2 % 2
                if stage == 0:
                    v.wait_ge(s_gld[s2], 16 * (t2 // 2 + 1))
                    v.tensor_reduce(BASEr[:, :].unsqueeze(2),
                                    DW[s2][:, :].rearrange("p (b f) -> p b f", f=64),
                                    AX.X, A.max, apply_absolute_value=True)
                elif stage == 1:
                    v.tensor_scalar(BASE[s2][:, :], BASEr[:, :], 1e-8, 1.0 / 15.0,
                                    A.max, A.mult)
                else:
                    v.reciprocal(RBSP[s2][:, :], BASE[s2][:, :])

            for tile in range(32):
                sl = tile % 2
                if tile == 0:
                    pre_chain(0, 0)
                    sp()
                    pre_chain(0, 1)
                    sp()
                    pre_chain(0, 2)
                    sp()
                else:
                    v.wait_ge(s_s2, sg2(tile - 1, SPT))
                v.tensor_tensor(DU[:, :].rearrange("p (b f) -> p b f", f=64),
                                DW[sl][:, :].rearrange("p (b f) -> p b f", f=64),
                                RBSP[sl][:, :].unsqueeze(2).broadcast_to([128, 16, 64]),
                                A.mult).then_inc(s_v, 1)

                def blend(bk, piece):
                    if bk < 0:
                        return
                    if bk == 0:
                        if piece == 0:
                            v.tensor_copy(BER[:, :], ERR[0][:, :])
                        elif piece == 1:
                            v.tensor_scalar(BSS[:, :], ONES[:, 0:16], SHRINKS[0],
                                            None, A.mult)
                        return
                    eb = ERR[bk % 2]
                    if piece == 0:
                        v.tensor_tensor(UPD[:, :], eb[:, :], BER[:, :], A.is_lt)
                        v.tensor_tensor(TD3[:, :], eb[:, :], BER[:, :], A.subtract)
                    elif piece == 1:
                        v.tensor_tensor(TD3[:, :], UPD[:, :], TD3[:, :], A.mult)
                        v.tensor_tensor(TD1[:, :], UPD[:, :], BSS[:, :], A.mult)
                    elif piece == 2:
                        v.tensor_tensor(BER[:, :], BER[:, :], TD3[:, :], A.add)
                    elif piece == 3:
                        v.scalar_tensor_tensor(TD2[:, :], UPD[:, :], SHRINKS[bk],
                                               TD1[:, :], A.mult, A.subtract)
                    else:
                        v.tensor_tensor(BSS[:, :], BSS[:, :], TD2[:, :], A.add)

                for k in range(18):
                    bk = k - 3
                    if k <= 15:
                        v.wait_ge(s_s2, sg2(tile, z_inc(k)))
                        v.tensor_scalar(DY[k % 3][:, :], DY[k % 3][:, :],
                                        QCLIP + 0.5, -QCLIP + 0.5,
                                        A.min, A.max).then_inc(s_v, 1)
                    if k < 2:
                        continue
                    kc_ = k - 2
                    kr = kc_ % 3
                    blend(bk, 0)
                    v.wait_ge(s_g, gg(tile, 2 + kc_))
                    v.tensor_tensor(DC[:, :], DQI[kr][:, :], DY[kr][:, :], A.is_gt)
                    blend(bk, 1)
                    v.tensor_tensor(DC[:, :], DQI[kr][:, :], DC[:, :], A.subtract)
                    blend(bk, 2)
                    if kc_ >= 1:
                        v.wait_ge(s_s2, sg2(tile, sq_inc(kc_ - 1)))
                    v.scalar_tensor_tensor(DE[:, :], DC[:, :], SHRINKS[kc_],
                                           DU[:, :], A.mult,
                                           A.subtract).then_inc(s_v, 1)
                    blend(bk, 3)
                    if k == 9 and tile < 31:
                        pre_chain(tile + 1, 0)
                    if k == 11 and tile < 31:
                        pre_chain(tile + 1, 1)
                    if k == 13 and tile < 31:
                        pre_chain(tile + 1, 2)
                    v.wait_ge(s_s2, sg2(tile, sq_inc(kc_)))
                    v.tensor_reduce(ERR[kc_ % 2][:, :].unsqueeze(2),
                                    DE[:, :].rearrange("p (b f) -> p b f", f=64),
                                    AX.X, A.add).then_inc(s_v, 1)
                    if bk >= 1 and k == 17:
                        sp()
                    blend(bk, 4)

                sp()
                v.tensor_tensor(UPD[:, :], ERR[1][:, :], BER[:, :], A.is_lt)
                sp()
                v.tensor_tensor(TD1[:, :], UPD[:, :], BSS[:, :], A.mult)
                sp()
                v.scalar_tensor_tensor(TD2[:, :], UPD[:, :], SHRINKS[15],
                                       TD1[:, :], A.mult, A.subtract)
                sp()
                v.tensor_tensor(BSS[:, :], BSS[:, :], TD2[:, :], A.add)
                sp()
                v.reciprocal(RBSF[:, :], BSS[:, :])
                sp()
                v.wait_ge(s_g, gg(tile, GPT))
                v.tensor_tensor(DY[0][:, :].rearrange("p (b f) -> p b f", f=64),
                                DU[:, :].rearrange("p (b f) -> p b f", f=64),
                                RBSF[:, :].unsqueeze(2).broadcast_to([128, 16, 64]),
                                A.mult)
                v.tensor_scalar(DY[0][:, :], DY[0][:, :], QCLIP, -QCLIP,
                                A.min, A.max)
                v.tensor_scalar(DY[0][:, :], DY[0][:, :], 0.5, None,
                                A.add).then_inc(s_v, 1)
                v.tensor_copy(DQI[0][:, :], DY[0][:, :])
                v.tensor_copy(DY[1][:, :], DQI[0][:, :])
                v.tensor_tensor(DC[:, :], DY[0][:, :], DY[1][:, :], A.subtract)
                v.tensor_scalar(DE[:, :], DC[:, :], 1.0, None, A.is_ge)
                v.tensor_tensor(DY[1][:, :], DY[1][:, :], DE[:, :], A.add)
                v.tensor_scalar(DE[:, :], DC[:, :], 2.0, None, A.is_ge)
                v.tensor_tensor(DY[1][:, :], DY[1][:, :], DE[:, :], A.add)
                v.tensor_scalar(DE[:, :], DC[:, :], 0.0, None, A.is_lt)
                v.tensor_tensor(DY[1][:, :], DY[1][:, :], DE[:, :], A.subtract)
                v.tensor_scalar(DE[:, :], DC[:, :], -1.0, None, A.is_lt)
                v.tensor_tensor(DY[1][:, :], DY[1][:, :], DE[:, :], A.subtract)
                v.tensor_tensor(TD1[:, :], BASE[sl][:, :], BSS[:, :], A.mult)
                sp()
                if tile >= 2:
                    v.wait_ge(s_gst[sl], 16 * (tile // 2))
                v.tensor_tensor(DDQ[sl][:, :].rearrange("p (b f) -> p b f", f=64),
                                DY[1][:, :].rearrange("p (b f) -> p b f", f=64),
                                TD1[:, :].unsqueeze(2).broadcast_to([128, 16, 64]),
                                A.mult).then_inc(s_v, 1)
                n["v"] += VPT
        n["g"] += 32 * GPT

        @block.scalar
        def _(s):
            next_oc = [0]

            def emit_evacs(upto):
                while next_oc[0] < min(upto, NOC):
                    oc = next_oc[0]
                    s.wait_ge(s_t, t_p1 + oc + 1)
                    if oc >= 1:
                        s.wait_ge(s_cdma, cd_p1 + 16 * oc)
                    s.activation(QS[:, :], QP[:, :], ACT.Identity,
                                 bias=BQ[:, oc:oc + 1]).then_inc(s_s, 1)
                    n["s"] += 1
                    s.wait_ge(s_s, n["s"])
                    s.dma_start(out=qkv_scr[oc], in_=QS[:, :]).then_inc(s_cdma, 16)
                    n["cdma"] += 16
                    next_oc[0] += 1

            def zk(tile, k):
                s.activation(DY[k % 3][:, :], DU[:, :], ACT.Copy,
                             bias=0.5, scale=1.0 / SHRINKS[k]).then_inc(s_s2, 1)

            for tile in range(32):
                s.wait_ge(s_v, vg(tile, 1))
                for k3 in range(3):
                    if tile >= 1:
                        s.wait_ge(s_g, gg(tile - 1, 15 + k3))
                    zk(tile, k3)
                for k in range(13):
                    emit_evacs(((tile * 16 + k) * 48) // 512)
                    s.wait_ge(s_v, vg(tile, v_e(k)))
                    if k >= 1:
                        s.wait_ge(s_v, vg(tile, v_red(k - 1)))
                    s.activation(DE[:, :], DE[:, :], ACT.Square).then_inc(s_s2, 1)
                    s.wait_ge(s_g, gg(tile, 2 + k))
                    zk(tile, k + 3)
                for k in (13, 14, 15):
                    emit_evacs(((tile * 16 + k) * 48) // 512)
                    s.wait_ge(s_v, vg(tile, v_e(k)))
                    s.wait_ge(s_v, vg(tile, v_red(k - 1)))
                    s.activation(DE[:, :], DE[:, :], ACT.Square).then_inc(s_s2, 1)
            emit_evacs(NOC)

        barrier()
        p1.close()

        if upto == "p1d":
            dump_and_done([deq_nat[t2].rearrange("p f -> p f") for t2 in range(16)])
            p02.close()
            return nc

        if upto == "p1":
            sel = [0, 1, 2, 3, 16, 17, 18, 19, 32, 33, 34, 35, 12, 28, 44, 47]
            dump_and_done([qkv_scr[c] for c in sel])
            p02.close()
            return nc

        _sc.__exit__(None, None, None)
        _sc = nc.named_scope("P15"); _sc.__enter__()
        p15 = ExitStack()
        DNT = [p15.enter_context(nc.sbuf_tensor(f"DNT{i}", [128, 1024], F32)) for i in range(2)]
        DST = [p15.enter_context(nc.sbuf_tensor(f"DST{i}", [128, 1024], F32)) for i in range(2)]
        DTP = p15.enter_context(nc.psum_tensor("DTP", [128, 1024], F32))

        def slab_off(tile):
            if tile < 16:
                r, sct = tile // 2, tile % 2
                return r * FCT + sct * (128 * 1024)
            tt2 = tile - 16
            r, sct = tt2 // 8, tt2 % 8
            return 8 * FCT + r * PJT + sct * (128 * 1024)

        la_p15 = n["la"]
        lb_p15 = n["lb"]
        t_p15 = n["t"]
        v_p15 = n["v"]
        gd_p15 = n["gdma"]

        @block.sync
        def _(sy):
            for tile in range(32):
                sl = tile % 2
                ssem = s_la if sl == 0 else s_lb
                if tile >= 2:
                    sy.wait_ge(s_t, t_p15 + tile - 1)
                sy.wait_ge(s_gst[sl], 16 * (tile // 2 + 1))
                sy.dma_start(out=DNT[sl][:, :], in_=deq_nat[tile]).then_inc(ssem, 16)
                n["la" if sl == 0 else "lb"] += 16

        @block.tensor
        def _(t):
            for tile in range(32):
                sl = tile % 2
                ssem = s_la if sl == 0 else s_lb
                base = la_p15 if sl == 0 else lb_p15
                t.wait_ge(ssem, base + 16 * (tile // 2 + 1))
                if tile >= 1:
                    t.wait_ge(s_v, v_p15 + tile)
                last = None
                for b in range(8):
                    last = t.transpose(DTP[:, b * 128:(b + 1) * 128],
                                       DNT[sl][:, b * 128:(b + 1) * 128], IDENT[:, :])
                last.then_inc(s_t, 1)
        n["t"] += 32

        @block.vector
        def _(v):
            for tile in range(32):
                sl = tile % 2
                v.wait_ge(s_t, t_p15 + tile + 1)
                if tile >= 2:
                    v.wait_ge(s_gst[sl], 256 + 16 * (tile // 2))
                v.tensor_copy(DST[sl][:, :], DTP[:, :]).then_inc(s_v, 1)
                n["v"] += 1

        @block.gpsimd
        def _(g):
            for tile in range(32):
                sl = tile % 2
                g.wait_ge(s_v, v_p15 + tile + 1)
                off = slab_off(tile)
                dst_t = deq_in_fc if tile < 16 else deq_in_pj
                if tile >= 16:
                    off -= 8 * FCT
                g.dma_start(
                    out=dst_t[0, off:off + 128 * 1024].rearrange("(p f) -> p f", p=128),
                    in_=DST[sl][:, :],
                ).then_inc(s_gst[sl], 16)
                if tile == 15:
                    g.wait_ge(s_gst[0], 256 + 16 * 8)
                    g.wait_ge(s_gst[1], 256 + 16 * 8)
                    g.collective_compute(
                        "AllGather", A.bypass,
                        replica_groups=[list(range(NCORES))],
                        ins=[deq_in_fc[:, :]],
                        outs=[deq_all_fc[:, :]],
                    ).then_inc(s_cc, 1)
            g.wait_ge(s_gst[0], 256 + 16 * 16)
            g.wait_ge(s_gst[1], 256 + 16 * 16)
            g.sem_inc(s_g, 1)
            if upto == "p15":
                g.collective_compute(
                    "AllGather", A.bypass,
                    replica_groups=[list(range(NCORES))],
                    ins=[deq_in_pj[:, :]],
                    outs=[deq_all_pj[:, :]],
                ).then_inc(s_cc, 1)
                g.wait_ge(s_cc, 2)
        n["g"] += 1
        g_pjrdy = n["g"]

        barrier()
        p15.close()

        if upto == "p15":
            dump_and_done([deq_fc_f[j * 262144:(j + 1) * 262144]
                           .rearrange("(p f) -> p f", p=128) for j in range(16)])
            p02.close()
            return nc

        _sc.__exit__(None, None, None)
        _sc = nc.named_scope("P2"); _sc.__enter__()
        p2 = ExitStack()
        FCW = [p2.enter_context(nc.sbuf_tensor(f"FCW{i}", [128, NIC * 128], F32)) for i in range(2)]
        FR = [p2.enter_context(nc.sbuf_tensor(f"FR{i}", [128, T], F32)) for i in range(2)]
        FH = [p2.enter_context(nc.sbuf_tensor(f"FH{i}", [128, T], F32)) for i in range(2)]
        FCP = [p2.enter_context(nc.psum_tensor(f"FCP{i}", [128, T], F32)) for i in range(2)]

        la_p2 = n["la"]
        lb_p2 = n["lb"]
        t_p2 = n["t"]
        s_p2 = n["s"]
        v_p2 = n["v"]
        cd_p2 = n["cdma"]

        def fc_tile_ap(jc):
            c, r = jc // 8, jc % 8
            base = c * FCS + r * FCT
            return deq_fc_f[base:base + FCT].rearrange("(s p f) -> p s f",
                                                       s=2, p=128)

        @block.sync
        def _(sy):
            sy.wait_ge(s_cc, 1)
            for jc in range(NJC):
                sl = jc % 2
                ssem = s_la if sl == 0 else s_lb
                if jc >= 2:
                    sy.wait_ge(s_t, t_p2 + jc - 1)
                sy.dma_start(
                    out=FCW[sl][:, :].rearrange("p (s f) -> p s f", s=2),
                    in_=fc_tile_ap(jc),
                ).then_inc(ssem, 16)
                n["la" if sl == 0 else "lb"] += 16

        @block.tensor
        def _(t):
            for jc in range(NJC):
                sl = jc % 2
                ssem = s_la if sl == 0 else s_lb
                base = la_p2 if sl == 0 else lb_p2
                t.wait_ge(ssem, base + 16 * (jc // 2 + 1))
                if jc >= 2:
                    t.wait_ge(s_s, s_p2 + jc - 1)
                last = None
                for tt in range(4):
                    for ic in range(NIC):
                        last = t.matmul(FCP[sl][:, tt * 512:(tt + 1) * 512],
                                 lhsT=FCW[sl][:, ic * 128:(ic + 1) * 128],
                                 rhs=XN[:, ic * T + tt * 512: ic * T + (tt + 1) * 512],
                                 start=(ic == 0), stop=(ic == NIC - 1))
                last.then_inc(s_t, 1)
        n["t"] += NJC

        @block.scalar
        def _(s):
            for jc in range(NJC):
                sl = jc % 2
                s.wait_ge(s_t, t_p2 + jc + 1)
                if jc >= 2:
                    s.wait_ge(s_v, v_p2 + jc - 1)
                s.activation(FR[sl][:, :], FCP[sl][:, :], ACT.Relu).then_inc(s_s, 1)
                n["s"] += 1
                s.wait_ge(s_v, v_p2 + jc + 1)
                s.dma_start(out=h2_scr[jc], in_=FH[sl][:, :]).then_inc(s_stp[sl], 16)

        @block.vector
        def _(v):
            for jc in range(NJC):
                sl = jc % 2
                v.wait_ge(s_s, s_p2 + jc + 1)
                if jc >= 2:
                    v.wait_ge(s_stp[sl], 16 * (jc // 2))
                v.tensor_tensor(FH[sl][:, :], FR[sl][:, :], FR[sl][:, :],
                                A.mult).then_inc(s_v, 1)
                n["v"] += 1

        @block.gpsimd
        def _(g):
            g.wait_ge(s_g, g_pjrdy)
            g.wait_ge(s_t, t_p2 + NJC)
            g.collective_compute(
                "AllGather", A.bypass,
                replica_groups=[list(range(NCORES))],
                ins=[deq_in_pj[:, :]],
                outs=[deq_all_pj[:, :]],
            ).then_inc(s_cc, 1)

        barrier()
        p2.close()
        p02.close()

        if upto == "p2":
            dump_and_done([h2_scr[j] for j in range(16)])
            return nc

        _sc.__exit__(None, None, None)
        _sc = nc.named_scope("P3"); _sc.__enter__()
        p3 = ExitStack()
        QH = [p3.enter_context(nc.sbuf_tensor(f"QH{i}", [128, T], F32)) for i in range(2)]
        KH = [p3.enter_context(nc.sbuf_tensor(f"KH{i}", [128, T], F32)) for i in range(2)]
        VH = [p3.enter_context(nc.sbuf_tensor(f"VH{i}", [128, T], F32)) for i in range(2)]
        VTM = p3.enter_context(nc.sbuf_tensor("VTM", [128, T], F32))
        PTS = p3.enter_context(nc.sbuf_tensor("PTS", [128, T], F32))
        AOU = p3.enter_context(nc.sbuf_tensor("AOU", [128, T], F32))
        AOH = p3.enter_context(nc.sbuf_tensor("AOH", [128, T], F32))
        DISC = p3.enter_context(nc.sbuf_tensor("DISC", [128, T], F32))
        MCOL = p3.enter_context(nc.sbuf_tensor("MCOL", [128, 1], F32))
        MBF = p3.enter_context(nc.sbuf_tensor("MBF", [128, 1], BF16))
        BIASC = p3.enter_context(nc.sbuf_tensor("BIASC", [128, 1], F32))
        DM = p3.enter_context(nc.sbuf_tensor("DM", [128, 128], BF16))
        DALL = p3.enter_context(nc.sbuf_tensor("DALL", [128, T], F32))
        L16 = p3.enter_context(nc.sbuf_tensor("L16", [128, 16], F32))
        RL16 = p3.enter_context(nc.sbuf_tensor("RL16", [128, 16], F32))
        SP = p3.enter_context(nc.psum_tensor("SP", [128, T], F32))
        STP = p3.enter_context(nc.psum_tensor("STP", [128, T], F32))

        la_p3 = n["la"]
        lb_p3 = n["lb"]
        t_p3 = n["t"]
        t2_p3 = n["t2"]
        v_p3 = n["v"]
        v2_p3 = n["v2"]
        s_p3 = n["s"]
        cd_p3 = n["cdma"]

        def tS(h, qt):
            return t_p3 + 48 * h + (1 if qt == 0 else 3 * qt)

        def tST(h, qt):
            return t_p3 + 48 * h + (3 * qt + 2 if qt <= 14 else 47)

        def tPV(h, qt):
            return t_p3 + 48 * h + (3 * qt + 4 if qt <= 14 else 48)

        def v_qt(h, qt, j):
            return v_p3 + (h * 16 + qt) * 2 + j

        def s_qt(h, qt, j):
            return s_p3 + (h * 16 + qt) * 2 + j

        @block.sync
        def _(sy):
            for h in range(NH):
                sl = h % 2
                ssem = s_la if sl == 0 else s_lb
                if h >= 2:
                    sy.wait_ge(s_t, tPV(h - 2, 15))
                sy.dma_start(out=QH[sl][:, :], in_=qkv_scr[h]).then_inc(ssem, 16)
                sy.dma_start(out=KH[sl][:, :], in_=qkv_scr[16 + h]).then_inc(ssem, 16)
                sy.dma_start(out=VH[sl][:, :], in_=qkv_scr[32 + h]).then_inc(ssem, 16)
                n["la" if sl == 0 else "lb"] += 48

        @block.tensor
        def _(t):
            for h in range(NH):
                sl = h % 2
                ssem = s_la if sl == 0 else s_lb
                base = la_p3 if sl == 0 else lb_p3
                t.wait_ge(ssem, base + 48 * (h // 2 + 1))
                if h >= 1:
                    t.wait_ge(s_s, s_qt(h - 1, 15, 2))
                    t.wait_ge(s_v2, v2_p3 + 4 * h)
                last = None
                for b in range(NIC):
                    last = t.transpose(STP[:, b * 128:(b + 1) * 128],
                                       VH[sl][:, b * 128:(b + 1) * 128], IDENT[:, :])
                last.then_inc(s_t2, 1)
                last = None
                for kk in range(4):
                    last = t.matmul(SP[:, kk * 512:(kk + 1) * 512],
                             lhsT=QH[sl][:, 0:128],
                             rhs=KH[sl][:, kk * 512:(kk + 1) * 512],
                             start=True, stop=True)
                last.then_inc(s_t, 1)
                for qt in range(16):
                    t.wait_ge(s_v, v_qt(h, qt, 1))
                    if qt == 0:
                        t.wait_ge(s_v2, v2_p3 + 4 * h + 1)
                    else:
                        t.wait_ge(s_s, s_qt(h, qt - 1, 2))
                        t.wait_ge(s_v, v_qt(h, qt - 1, 2))
                    for kc in range(NIC):
                        t.matmul(STP[:, kc * 128:(kc + 1) * 128],
                                 lhsT=KH[sl][:, kc * 128:(kc + 1) * 128],
                                 rhs=QH[sl][:, qt * 128:(qt + 1) * 128],
                                 start=(kc % 4 == 0), stop=False)
                    last = None
                    for kc in range(NIC):
                        last = t.matmul(STP[:, kc * 128:(kc + 1) * 128],
                                 lhsT=ONESB2[:, :], rhs=DM[:, :],
                                 start=False, stop=(kc % 4 == 3))
                    last.then_inc(s_t, 1)
                    if qt < 15:
                        t.wait_ge(s_s, s_qt(h, qt, 1))
                        last = None
                        for kk in range(4):
                            last = t.matmul(SP[:, kk * 512:(kk + 1) * 512],
                                     lhsT=QH[sl][:, (qt + 1) * 128:(qt + 2) * 128],
                                     rhs=KH[sl][:, kk * 512:(kk + 1) * 512],
                                     start=True, stop=True)
                        last.then_inc(s_t, 1)
                    t.wait_ge(s_s, s_qt(h, qt, 2))
                    last = None
                    for kc in range(NIC):
                        last = t.matmul(STP[:, 0:128],
                                 lhsT=VTM[:, kc * 128:(kc + 1) * 128],
                                 rhs=PTS[:, kc * 128:(kc + 1) * 128],
                                 start=(kc == 0), stop=(kc == NIC - 1))
                    last.then_inc(s_t, 1)
                t.wait_ge(s_v2, v2_p3 + 4 * h + 3)
                t.wait_ge(s_s, s_qt(h, 15, 2))
                last = None
                for tt in range(4):
                    last = t.matmul(SP[:, tt * 512:(tt + 1) * 512],
                             lhsT=ONES[:, :],
                             rhs=DALL[:, tt * 512:(tt + 1) * 512],
                             start=True, stop=True)
                last.then_inc(s_t2, 1)
        n["t"] += NH * 48
        n["t2"] += NH * 2
        assert n["t"] == tPV(NH - 1, 15), (n["t"], tPV(NH - 1, 15))

        @block.vector
        def _(v):
            for h in range(NH):
                v.wait_ge(s_t2, t2_p3 + 2 * h + 1)
                v.tensor_copy(VTM[:, :], STP[:, :]).then_inc(s_v2, 1)
                for qt in range(16):
                    v.wait_ge(s_t, tS(h, qt))
                    if h + qt > 0:
                        pqh, pqt = (h, qt - 1) if qt > 0 else (h - 1, 15)
                        v.wait_ge(s_s, s_qt(pqh, pqt, 1))
                        v.wait_ge(s_t, tST(pqh, pqt))
                    v.tensor_reduce(MCOL[:, :], SP[:, :], AX.X, A.max)
                    v.tensor_copy(SPC[:, :], IDENT[:, :])
                    v.tensor_copy(MBF[:, :], MCOL[:, :])
                    v.tensor_copy(SPC[:, :], IDENT[:, :])
                    v.tensor_scalar(BIASC[:, :], MBF[:, :], -CSC, None, A.mult)
                    v.scalar_tensor_tensor(DM[:, :],
                                           MBF[:, 0:1].broadcast_to([128, 128]),
                                           -1.0, IDENTB[:, :], A.mult,
                                           A.mult).then_inc(s_v, 1)
                    v.wait_ge(s_t, tPV(h, qt))
                    v.tensor_copy(AOU[:, qt * 128:(qt + 1) * 128],
                                  STP[:, 0:128]).then_inc(s_v, 1)
                v.wait_ge(s_s, s_qt(h, 15, 2))
                v.reciprocal(RL16[:, :], L16[:, :]).then_inc(s_v2, 1)
                v.tensor_copy(SPC[:, :], IDENT[:, :])
                if h >= 1:
                    v.wait_ge(s_t2, t2_p3 + 2 * h)
                last = None
                for qt in range(16):
                    last = v.scalar_tensor_tensor(DALL[:, qt * 128:(qt + 1) * 128],
                                           RL16[:, qt:qt + 1].broadcast_to([128, 128]),
                                           0.0, IDENT[:, :], A.bypass, A.mult)
                last.then_inc(s_v2, 1)
                v.wait_ge(s_t2, t2_p3 + 2 * h + 2)
                if h >= 1:
                    v.wait_ge(s_cdma, cd_p3 + 16 * h)
                v.tensor_tensor(AOH[:, :], AOU[:, :], SP[:, :], A.mult).then_inc(s_v2, 1)
        n["v"] += NH * 32
        n["v2"] += NH * 4

        @block.scalar
        def _(s):
            for h in range(NH):
                for qt in range(16):
                    s.wait_ge(s_t, tS(h, qt))
                    s.wait_ge(s_v, v_qt(h, qt, 1))
                    s.activation(DISC[:, :], SP[:, :], ACT.Exp,
                                 bias=BIASC[:, 0:1], scale=CSC,
                                 accum_out=L16[:, qt:qt + 1]).then_inc(s_s, 1)
                    n["s"] += 1
                    s.wait_ge(s_t, tST(h, qt))
                    s.activation(PTS[:, :], STP[:, :], ACT.Exp,
                                 scale=CSC).then_inc(s_s, 1)
                    n["s"] += 1
                s.wait_ge(s_v2, v2_p3 + 4 * (h + 1))
                s.dma_start(out=ao_scr[h], in_=AOH[:, :]).then_inc(s_cdma, 16)
                n["cdma"] += 16

        barrier()
        p3.close()

        if upto == "p3":
            dump_and_done([ao_scr[h] for h in range(16)])
            return nc

        _sc.__exit__(None, None, None)
        _sc = nc.named_scope("P35"); _sc.__enter__()
        p35 = ExitStack()
        AOSB = p35.enter_context(nc.sbuf_tensor("AOSB", [128, 16 * 512], F32))
        WPJ = [p35.enter_context(nc.sbuf_tensor(f"WPJ{i}", [128, NIC * 128], F32)) for i in range(2)]
        PST = [p35.enter_context(nc.sbuf_tensor(f"PST{i}", [128, 512], F32)) for i in range(2)]
        PPP = [p35.enter_context(nc.psum_tensor(f"PPP{i}", [128, 512], F32)) for i in range(2)]

        la_p35 = n["la"]
        lb_p35 = n["lb"]
        lc_p35 = n["lc"]
        t_p35 = n["t"]
        s_p35 = n["s"]
        cd_p35 = n["cdma"]

        @block.sync
        def _(sy):
            for ts in range(4):
                if ts >= 1:
                    sy.wait_ge(s_t, t_p35 + ts * 16)
                for h in range(NH):
                    sy.dma_start(out=AOSB[:, h * 512:(h + 1) * 512],
                                 in_=ao_scr[h][:, ts * 512:(ts + 1) * 512]
                                 ).then_inc(s_lc, 16)
                    n["lc"] += 16
                for oc in range(16):
                    it = ts * 16 + oc
                    sl = it % 2
                    ssem = s_la if sl == 0 else s_lb
                    if it >= 2:
                        sy.wait_ge(s_t, t_p35 + it - 1)
                    sy.dma_start(
                        out=WPJ[sl][:, :].rearrange("p (c m) -> p c m", m=128),
                        in_=wprojt[oc].rearrange("(c p) m -> p c m", p=128),
                    ).then_inc(ssem, 16)
                    n["la" if sl == 0 else "lb"] += 16

        @block.tensor
        def _(t):
            for ts in range(4):
                t.wait_ge(s_lc, lc_p35 + 256 * (ts + 1))
                for oc in range(16):
                    it = ts * 16 + oc
                    sl = it % 2
                    ssem = s_la if sl == 0 else s_lb
                    base = la_p35 if sl == 0 else lb_p35
                    t.wait_ge(ssem, base + 16 * (it // 2 + 1))
                    if it >= 2:
                        t.wait_ge(s_s, s_p35 + it - 1)
                    last = None
                    for ad in range(NIC):
                        last = t.matmul(PPP[sl][:, :],
                                 lhsT=WPJ[sl][:, ad * 128:(ad + 1) * 128],
                                 rhs=AOSB[:, ad * 512:(ad + 1) * 512],
                                 start=(ad == 0), stop=(ad == NIC - 1))
                    last.then_inc(s_t, 1)
        n["t"] += 64

        @block.scalar
        def _(s):
            for it in range(64):
                ts, oc = it // 16, it % 16
                sl = it % 2
                s.wait_ge(s_t, t_p35 + it + 1)
                if it >= 2:
                    s.wait_ge(s_stp[sl], 32 * 16 + 16 * (it // 2))
                s.activation(PST[sl][:, :], PPP[sl][:, :], ACT.Identity,
                             bias=BP[:, oc:oc + 1]).then_inc(s_s, 1)
                n["s"] += 1
                s.wait_ge(s_s, n["s"])
                s.dma_start(out=ap_scr[oc][:, ts * 512:(ts + 1) * 512],
                            in_=PST[sl][:, :]).then_inc(s_stp[sl], 16)

        barrier()
        p35.close()

        if upto == "p35":
            dump_and_done([ap_scr[oc] for oc in range(16)])
            return nc

        _sc.__exit__(None, None, None)
        _sc = nc.named_scope("P4"); _sc.__enter__()
        p4 = ExitStack()
        H2S = p4.enter_context(nc.sbuf_tensor("H2S", [128, NJC * 512], F32))
        WDQ = [p4.enter_context(nc.sbuf_tensor(f"WDQ{i}", [128, 16 * 128], F32)) for i in range(2)]
        XT = [p4.enter_context(nc.sbuf_tensor(f"XT{i}", [128, 512], F32)) for i in range(2)]
        APT = [p4.enter_context(nc.sbuf_tensor(f"APT{i}", [128, 512], F32)) for i in range(2)]
        OST = [p4.enter_context(nc.sbuf_tensor(f"OST{i}", [128, 512], F32)) for i in range(2)]
        P4P = [p4.enter_context(nc.psum_tensor(f"P4P{i}", [128, 512], F32)) for i in range(2)]

        la_p4 = n["la"]
        lb_p4 = n["lb"]
        lc_p4 = n["lc"]
        t_p4 = n["t"]
        t2_p4 = n["t2"]
        v_p4 = n["v"]
        cd_p4 = n["cdma"]

        def pj_quarter_ap(oc, jq):
            c, r = oc // 2, oc % 2
            base = c * PJS + r * PJT + jq * (2 * 128 * 1024)
            return deq_pj_f[base:base + 2 * 128 * 1024].rearrange(
                "(s p f) -> p s f", s=2, p=128)

        def lc_ts(ts):
            return lc_p4 + (ts + 1) * 1024

        @block.sync
        def _(sy):
            sy.wait_ge(s_cc, 2)
            for ts in range(4):
                if ts >= 1:
                    sy.wait_ge(s_t2, t2_p4 + 64 * ts)
                for jc in range(NJC):
                    sy.dma_start(out=H2S[:, jc * 512:(jc + 1) * 512],
                                 in_=h2_scr[jc][:, ts * 512:(ts + 1) * 512]
                                 ).then_inc(s_lc, 16)
                    n["lc"] += 16
                for oc in range(16):
                    it = ts * 16 + oc
                    sl2 = it % 2
                    if it >= 2:
                        sy.wait_ge(s_v, v_p4 + it - 1)
                    sy.dma_start(out=XT[sl2][:, :],
                                 in_=x_dt[oc * 128:(oc + 1) * 128,
                                          ts * 512:(ts + 1) * 512]).then_inc(s_lx[sl2], 16)
                    sy.dma_start(out=APT[sl2][:, :],
                                 in_=ap_scr[oc][:, ts * 512:(ts + 1) * 512]
                                 ).then_inc(s_lx[sl2], 16)
                    for jq in range(4):
                        sq = it * 4 + jq
                        sl = sq % 2
                        ssem = s_la if sl == 0 else s_lb
                        if sq >= 2:
                            sy.wait_ge(s_t2, t2_p4 + sq - 1)
                        sy.dma_start(
                            out=WDQ[sl][:, :].rearrange("p (s f) -> p s f", s=2),
                            in_=pj_quarter_ap(oc, jq),
                        ).then_inc(ssem, 16)
                        n["la" if sl == 0 else "lb"] += 16

        @block.tensor
        def _(t):
            for ts in range(4):
                t.wait_ge(s_lc, lc_ts(ts))
                for oc in range(16):
                    it = ts * 16 + oc
                    sl2 = it % 2
                    if it >= 2:
                        t.wait_ge(s_v, v_p4 + it - 1)
                    for jq in range(4):
                        sq = it * 4 + jq
                        sl = sq % 2
                        ssem = s_la if sl == 0 else s_lb
                        base = la_p4 if sl == 0 else lb_p4
                        t.wait_ge(ssem, base + 16 * (sq // 2 + 1))
                        last = None
                        for jj in range(16):
                            jcg = jq * 16 + jj
                            last = t.matmul(P4P[sl2][:, :],
                                     lhsT=WDQ[sl][:, jj * 128:(jj + 1) * 128],
                                     rhs=H2S[:, jcg * 512:(jcg + 1) * 512],
                                     start=(jq == 0 and jj == 0),
                                     stop=(jq == 3 and jj == 15))
                        last.then_inc(s_t2, 1)
                        n["t2"] += 1

        @block.vector
        def _(v):
            for ts in range(4):
                for oc in range(16):
                    it = ts * 16 + oc
                    sl2 = it % 2
                    v.wait_ge(s_t2, t2_p4 + 4 * (it + 1))
                    v.wait_ge(s_lx[sl2], 32 * (it // 2 + 1))
                    if it >= 2:
                        v.wait_ge(s_stp[sl2], 64 * 16 + 16 * (it // 2))
                    v.tensor_tensor(OST[sl2][:, :], P4P[sl2][:, :], APT[sl2][:, :],
                                    A.add)
                    v.tensor_tensor(OST[sl2][:, :], OST[sl2][:, :], XT[sl2][:, :],
                                    A.add).then_inc(s_v, 1)
                    n["v"] += 1

        @block.scalar
        def _(s):
            for it in range(64):
                ts, oc = it // 16, it % 16
                sl2 = it % 2
                s.wait_ge(s_v, v_p4 + it + 1)
                s.dma_start(out=out_dt[oc * 128:(oc + 1) * 128,
                                       ts * 512:(ts + 1) * 512],
                            in_=OST[sl2][:, :]).then_inc(s_stp[sl2], 16)
            s.wait_ge(s_stp[0], 64 * 16 + 16 * 32)
            s.wait_ge(s_stp[1], 64 * 16 + 16 * 32)

        p4.close()
        _sc.__exit__(None, None, None)
    return nc


_NC_CACHE = None


def kernel(L0_attn_query, L0_attn_key, L0_attn_value, L0_attn_Wqkv, L0_attn_bqkv,
           L0_attn_Wproj, L0_attn_bproj, L0_ffn_w_fc, L0_ffn_w_proj):
    global _NC_CACHE
    x = np.asarray(L0_attn_query, dtype=np.float32)
    Wqkv = np.asarray(L0_attn_Wqkv, dtype=np.float32)
    bq = np.asarray(L0_attn_bqkv, dtype=np.float32)
    Wproj = np.asarray(L0_attn_Wproj, dtype=np.float32)
    bp = np.asarray(L0_attn_bproj, dtype=np.float32)
    w_fc = np.asarray(L0_ffn_w_fc, dtype=np.float32)
    w_proj = np.asarray(L0_ffn_w_proj, dtype=np.float32)

    idx = np.arange(6144).reshape(16, 3, 128)
    perm = np.concatenate([idx[:, 0], idx[:, 1], idx[:, 2]], axis=0).reshape(-1)
    WqkvT_t = np.ascontiguousarray(
        Wqkv[perm].T.reshape(2048, 48, 128).transpose(1, 0, 2))
    bq_t = np.ascontiguousarray(bq[perm].reshape(48, 128))
    WprojT_t = np.ascontiguousarray(
        Wproj.T.reshape(2048, 16, 128).transpose(1, 0, 2))
    bp_t = np.ascontiguousarray(bp.reshape(16, 128))

    if _NC_CACHE is None:
        _NC_CACHE = build()
    nc = _NC_CACHE

    in_maps = []
    for c in range(NCORES):
        in_maps.append({
            "x_dt": np.ascontiguousarray(x[c].T),
            "wqkvt": WqkvT_t,
            "bqkv": bq_t,
            "wprojt": WprojT_t,
            "bproj": bp_t,
            "wfc_sl": np.ascontiguousarray(w_fc[c * 1024:(c + 1) * 1024]),
            "wpj_sl": np.ascontiguousarray(w_proj[c * 256:(c + 1) * 256]),
        })
    res = run_bass_kernel_spmd(nc, in_maps, core_ids=list(range(NCORES)))
    out = np.stack([np.ascontiguousarray(res.results[c]["out_dt"].T)
                    for c in range(NCORES)], axis=0)
    return out
